# version 02
# baseline (speedup 1.0000x reference)
import numpy as np
from contextlib import ExitStack

import concourse.bass as bass
import concourse.mybir as mybir
from concourse.bass_utils import run_bass_kernel_spmd
from concourse.alu_op_type import AluOpType as ALU

F32 = mybir.dt.float32
BF16 = mybir.dt.bfloat16
AF = mybir.ActivationFunctionType
AX = mybir.AxisListType.X

T = 2048
D = 1024
NTC = T // 128
MLP = 4096
NE = 8
NH_B = 16
C_STAB = 20.0
C_STAB_A = 10.0
EPS = 1e-5

SLAB_HALF = 65664


class Sched:
    def __init__(self):
        self.ops = []
        self.cnt = {"T": 0, "A": 0, "V": 0, "G": 0, "S": 0}

    def op(self, engine, fn, wait=(), inc=None, inc_k=1):
        if inc is not None:
            self.cnt[inc] += inc_k
        self.ops.append((engine, fn, list(wait), inc, inc_k))

    def wt(self, name):
        return (name, self.cnt[name])

    def barrier_vals(self):
        return [(k, v) for k, v in self.cnt.items() if v > 0]


def build_program(n_sub=4, dbg=False):
    nc = bass.Bass("TRN2", target_bir_lowering=False)

    xd = nc.dram_tensor("x", [T, D], F32, kind="ExternalInput").ap()
    Mqd = nc.dram_tensor("Mq", [D, 512], F32, kind="ExternalInput").ap()
    Mkd = nc.dram_tensor("Mk", [D, 512], F32, kind="ExternalInput").ap()
    Mvd = nc.dram_tensor("Mv", [D, D], F32, kind="ExternalInput").ap()
    wqd = nc.dram_tensor("wq", [D, D], F32, kind="ExternalInput").ap()
    wdownd = nc.dram_tensor("wdown", [D, 512], F32, kind="ExternalInput").ap()
    wkupd = nc.dram_tensor("wkup", [512, D], F32, kind="ExternalInput").ap()
    wvupd = nc.dram_tensor("wvup", [512, D], F32, kind="ExternalInput").ap()
    wod = nc.dram_tensor("wo", [D, D], F32, kind="ExternalInput").ap()
    wgated = nc.dram_tensor("wgate", [D, NE], F32, kind="ExternalInput").ap()
    bgated = nc.dram_tensor("bgate", [128, NE], F32, kind="ExternalInput").ap()
    W1d = nc.dram_tensor("W1", [NE, D, MLP], F32, kind="ExternalInput").ap()
    b1d = nc.dram_tensor("b1l", [128, NE * 32], F32, kind="ExternalInput").ap()
    W2d = nc.dram_tensor("W2", [NE, MLP, D], F32, kind="ExternalInput").ap()
    b2d = nc.dram_tensor("b2", [NE, D], F32, kind="ExternalInput").ap()
    identd = nc.dram_tensor("ident", [128, 128], F32, kind="ExternalInput").ap()
    onesd = nc.dram_tensor("onesm", [128, 128], F32, kind="ExternalInput").ap()
    proped = nc.dram_tensor("prope", [128, 128], F32, kind="ExternalInput").ap()
    rcosd = nc.dram_tensor("rcos", [128, T], F32, kind="ExternalInput").ap()
    rsind = nc.dram_tensor("rsin", [128, T], F32, kind="ExternalInput").ap()
    yd = nc.dram_tensor("y", [T, D], F32, kind="ExternalOutput").ap()

    xd_t = xd.rearrange("(tc p) d -> p tc d", p=128)
    yd_t = yd.rearrange("(tc p) d -> p tc d", p=128)

    es = ExitStack()

    def sb(name, shape, dtype):
        return es.enter_context(nc.sbuf_tensor(name, shape, dtype)).ap()

    x_res = sb("x_res", [128, NTC, D], F32)
    ident_bf = sb("ident_bf", [128, 128], BF16)
    ident_f = sb("ident_f", [128, 128], F32)
    ones_bf = sb("ones_bf", [128, 128], BF16)
    onesf = sb("onesf", [1, 64], F32)
    prope_sb = sb("prope_sb", [128, 128], BF16)
    b1_sb = sb("b1_sb", [128, NE * 32], F32)
    b2_sb = sb("b2_sb", [NE, D], F32)
    wgate_sb = sb("wgate_sb", [128, NE, NE], BF16)
    bgate_bc = sb("bgate_bc", [128, NE], F32)
    xn_tile = sb("xn_tile", [128, 2, D], BF16)
    stat = sb("stat", [128, 16], F32)
    lnst = sb("lnst", [128, NTC, 2], F32)
    logit_sb = sb("logit_sb", [128, NTC, NE], F32)
    gb1 = sb("gb1", [128, NTC, NE], F32)
    gb2 = sb("gb2", [128, NTC, NE], F32)
    gb3 = sb("gb3", [128, NTC, NE], F32)
    gexp_all = sb("gexp_all", [128, NTC, NE], F32)
    gmx = sb("gmx", [128, NTC], F32)
    gsum = sb("gsum", [128, NTC], F32)
    grp = sb("grp", [128, NTC], F32)
    coef_sb = sb("coef_sb", [128, NTC, NE], F32)

    slab = sb("slab", [128, SLAB_HALF], BF16)

    def carve_bf(off_bytes, shape):
        n = int(np.prod(shape))
        v = slab[:, off_bytes // 2: off_bytes // 2 + n]
        if len(shape) == 1:
            return v
        if len(shape) == 2:
            return v.rearrange("p (a b) -> p a b", a=shape[0], b=shape[1])
        return v.rearrange("p (a b c) -> p a b c", a=shape[0], b=shape[1], c=shape[2])

    def carve_f32_flat(off_bytes, n):
        v = slab[:, off_bytes // 2: off_bytes // 2 + 2 * n].bitcast(F32)
        return v

    A_xntc = carve_bf(0, [2, 8, 128])
    A_xnTg = carve_bf(4096, [8, 256])
    A_qfg = carve_bf(8192, [8, 256])
    A_kfg = carve_bf(12288, [8, 256])
    A_aring = carve_bf(16384, [2, T])
    A_Mv = carve_bf(32768, [8, 1024])
    A_Mq = carve_bf(49152, [8, 512])
    A_Mk = carve_bf(57344, [8, 512])
    A_vf = carve_bf(98304, [16, D])
    F_xnT = carve_bf(0, [8, T])
    F_w1 = carve_bf(32768, [8, 1024])
    F_w2 = carve_bf(49152, [2, 8, 1024])
    F_hT = carve_bf(81920, [8, T])
    F_coefT = carve_f32_flat(114688, T)
    B_cos = carve_bf(0, [1, T])
    B_sin = carve_bf(4096, [1, T])
    B_kr = carve_bf(8192, [8, T])
    B_v = carve_bf(40960, [16, D])
    B_latT = carve_bf(73728, [4, T])
    B_wd = carve_bf(90112, [8, 512])
    B_wku = carve_bf(98304, [4, 1024])
    B_wv = carve_bf(90112, [4, 1024])
    B_xn3T = carve_bf(106496, [8, 512])
    B_tmp = carve_bf(114688, [2, 512])
    B_wq = carve_bf(73728, [8, 1024])
    B_wo = carve_bf(90112, [8, 1024])
    B_qT = carve_bf(114688, [8, 512])
    B_At = carve_bf(122880, [2, 256])
    B_aoT = carve_bf(123904, [8, 256])
    B_rbx = carve_f32_flat(128000, 256)
    rrow_sb = carve_f32_flat(130048, 256)

    PS = es.enter_context(nc.psum_tensor("ps_all", [128, 4096], F32)).ap()

    def bank(i, n=1):
        return PS[:, i * 512:(i + n) * 512]

    def bank_bf(i):
        return PS.bitcast(BF16)[:, i * 1024:(i + 1) * 1024]

    sems = {k: es.enter_context(nc.semaphore("sem_" + k)) for k in ["T", "A", "V", "G", "S"]}

    sch = Sched()
    OP, WT = sch.op, sch.wt

    def dma_g(dst, src, wait=()):
        OP("G", lambda e, d=dst, s=src: e.dma_start(out=d, in_=s), wait=wait, inc="G", inc_k=16)

    def dma_s(dst, src, wait=()):
        OP("S", lambda e, d=dst, s=src: e.dma_start(out=d, in_=s), wait=wait, inc="S", inc_k=16)

    def mm_group(out, pairs, wait=()):
        n = len(pairs)
        for i, (l, r) in enumerate(pairs):
            OP("T", lambda e, o=out, l=l, r=r, st=(i == 0), sp=(i == n - 1):
               e.matmul(o, l, r, start=st, stop=sp),
               wait=(wait if i == 0 else ()), inc=("T" if i == n - 1 else None))

    def act(fn, wait=()):
        OP("A", fn, wait=wait, inc="A")

    def dve(fn, wait=()):
        OP("V", fn, wait=wait, inc="V")

    def barrier():
        vals = sch.barrier_vals()
        for eng in ["T", "A", "V", "G", "S"]:
            w = [(k, v) for k, v in vals if k != eng]
            OP(eng, lambda e: None, wait=w)

    def ring_guard(sem):
        return (sem, max(0, sch.cnt[sem] - 1))

    for i in range(4):
        dma_s(x_res[:, 4 * i:4 * i + 4, :], xd_t[:, 4 * i:4 * i + 4, :])
    x_loaded = WT("S")
    dve(lambda e: e.memset(stat[:, 14:15], EPS))
    dve(lambda e: e.memset(stat[:, 15:16], -C_STAB))
    dve(lambda e: e.memset(stat[:, 8:9], -C_STAB_A))
    dma_g(ident_bf, identd)
    dma_g(ident_f, identd)
    dma_g(ones_bf, onesd)
    dma_g(onesf, onesd[0:1, 0:64])
    dma_g(prope_sb, proped)
    dma_g(b1_sb, b1d)
    dma_g(b2_sb, b2d)
    dma_g(wgate_sb, wgated.rearrange("(c p) e -> p c e", p=128))
    dma_g(bgate_bc, bgated)
    dma_g(A_Mq, Mqd.rearrange("(c p) f -> p c f", p=128))
    dma_g(A_Mk, Mkd.rearrange("(c p) f -> p c f", p=128))
    dma_g(A_Mv, Mvd.rearrange("(c p) f -> p c f", p=128))
    w_loads0 = WT("G")

    def emit_ln(tc_list, dst_of, save_stats=False, use_saved=False, after_tile=None):
        for j, tc in enumerate(tc_list):
            ring = tc % 2
            if use_saved:
                rstd = lnst[:, tc, 0:1]
                nmr = lnst[:, tc, 1:2]
            else:
                rstd = stat[:, 0:1]
                nmr = stat[:, 1:2]
                s1 = stat[:, 2:3]
                sq = stat[:, 3:4]
                m = stat[:, 4:5]
                var = stat[:, 5:6]
                dve(lambda e, tc=tc, s1=s1: e.reduce_sum(s1, x_res[:, tc, :], axis=AX),
                    wait=([x_loaded] if j == 0 else ()))
                dve(lambda e, tc=tc, ring=ring, sq=sq: e.scalar_tensor_tensor(
                    xn_tile[:, ring, :], x_res[:, tc, :], 1.0, x_res[:, tc, :],
                    op0=ALU.bypass, op1=ALU.mult, accum_out=sq))
                dve(lambda e, s1=s1, m=m: e.tensor_scalar(m, s1, 1.0 / D, None, op0=ALU.mult))
                dve(lambda e, m=m, var=var: e.tensor_scalar(var, m, m[:, 0:1], None,
                                                            op0=ALU.mult))
                dve(lambda e, sq=sq, var=var: e.scalar_tensor_tensor(
                    var, sq, 1.0 / D, var, op0=ALU.mult, op1=ALU.subtract))
                act(lambda e, var=var: e.activation(stat[:, 6:7], var, AF.Ln,
                                                    bias=stat[:, 14:15], scale=1.0),
                    wait=[WT("V")])
                act(lambda e, rstd=rstd: e.activation(rstd, stat[:, 6:7], AF.Exp, bias=0.0, scale=-0.5))
                dve(lambda e, m=m, rstd=rstd, nmr=nmr: e.tensor_scalar(
                    nmr, m, rstd[:, 0:1], -1.0, op0=ALU.mult, op1=ALU.mult), wait=[WT("A")])
                if save_stats:
                    dve(lambda e, tc=tc, rstd=rstd: e.tensor_copy(lnst[:, tc, 0:1], rstd))
                    dve(lambda e, tc=tc, nmr=nmr: e.tensor_copy(lnst[:, tc, 1:2], nmr))
            act(lambda e, tc=tc, ring=ring, rstd=rstd, nmr=nmr: e.activation(
                xn_tile[:, ring, :], x_res[:, tc, :], AF.Identity,
                bias=nmr[:, 0:1], scale=rstd[:, 0:1]),
                wait=[WT("V")])
            a_apply = WT("A")
            dst, co = dst_of(tc)
            for dmc in range(8):
                OP("T", lambda e, ring=ring, dmc=dmc: e.transpose(
                    bank_bf(7)[:, dmc * 128:(dmc + 1) * 128],
                    xn_tile[:, ring, dmc * 128:(dmc + 1) * 128], ident_bf),
                   wait=([a_apply, WT("V")] if dmc == 0 else ()),
                   inc=("T" if dmc == 7 else None))
            tdone = WT("T")
            dve(lambda e, co=co, dst=dst: e.tensor_copy(
                dst[:, :, co:co + 128],
                bank_bf(7).rearrange("p (c f) -> p c f", c=8)),
                wait=[tdone])
            if after_tile is not None:
                after_tile(tc)

    def emit_vf(tc):
        ring = tc % 2
        cp_V = WT("V")
        for half in range(2):
            rb = (tc * 2 + half) % 2
            w = [cp_V, w_loads0] if half == 0 else []
            w.append(ring_guard("A"))
            mm_group(bank(rb),
                     [(A_xntc[:, ring, dmc, :], A_Mv[:, dmc, half * 512:(half + 1) * 512])
                      for dmc in range(8)], wait=w)
            mmil = WT("T")
            act(lambda e, tc=tc, half=half, rb=rb: e.activation(
                A_vf[:, tc, half * 512:(half + 1) * 512], bank(rb), AF.Copy,
                bias=0.0, scale=1.0), wait=[mmil])

    emit_ln(list(range(NTC)), lambda tc: (A_xntc[:, tc % 2], 0),
            save_stats=True, after_tile=emit_vf)
    p1_done_A = WT("A")
    p1_done_T = WT("T")

    if n_sub >= 1:
        if n_sub >= 2:
            for dmc in range(8):
                dma_g(F_w1[:, dmc, :], W1d[0][dmc * 128:(dmc + 1) * 128, 0:1024],
                      wait=([p1_done_T] if dmc == 0 else ()))

        mile_sum = 0
        for g in range(8):
            emit_ln([2 * g, 2 * g + 1], lambda tc: (A_xnTg, (tc % 2) * 128),
                    use_saved=True)
            lng_V = WT("V")
            for which, Msb, dstg, scale in ((0, A_Mq, A_qfg, 0.125),
                                            (1, A_Mk, A_kfg, 1.0)):
                for s in range(8):
                    rb = s % 2
                    w = [lng_V] if (which == 0 and s == 0) else []
                    w.append(ring_guard("A"))
                    mm_group(bank(rb)[0:64, 0:256],
                             [(Msb[:, dmc, 64 * s:64 * s + 64], A_xnTg[:, dmc, :])
                              for dmc in range(8)], wait=w)
                    mmil = WT("T")
                    act(lambda e, s=s, rb=rb, dstg=dstg, scale=scale: e.activation(
                        dstg[0:64, s, :], bank(rb)[0:64, 0:256], AF.Copy,
                        bias=0.0, scale=scale), wait=[mmil])
            qk_A = WT("A")
            mile_exp_h = {}
            mile_norm = {}

            def emit_av(j):
                jr = j % 2
                for rc in range(2):
                    for h2 in range(8):
                        bi = 4 + rc * 2 + h2 // 4
                        off = (h2 % 4) * 128
                        st = (j == 0) and (h2 % 4 == 0)
                        sp = (j == 15) and (h2 % 4 == 3)
                        w = [("V", mile_norm[j])] if (rc == 0 and h2 == 0) else ()
                        OP("T", lambda e, j=j, rc=rc, h2=h2, bi=bi, off=off, st=st,
                           sp=sp, jr=jr: e.matmul(
                            bank(bi)[:, off:off + 128],
                            A_aring[:, jr, 256 * h2 + 128 * rc: 256 * h2 + 128 * rc + 128],
                            A_vf[:, 2 * h2 + (j % 2), 128 * (j // 2):128 * (j // 2) + 128],
                            start=st, stop=sp),
                           wait=w, inc=("T" if (rc == 1 and h2 == 7) else None))

            for m in range(16):
                ring = m % 2
                tS = m // 2
                for s in range(8):
                    bi = s // 2
                    off = (s % 2) * 256
                    w = []
                    if s == 0 or s == 4:
                        half = s // 4
                        if m == 0:
                            w.append(("A", sch.cnt["A"]))
                        else:
                            w.append(("A", mile_exp_h[(m - 1, half)]))
                    OP("T", lambda e, m=m, s=s, bi=bi, off=off, tS=tS: e.matmul(
                        bank(bi)[:, off:off + 256],
                        A_kfg[0:64, tS, 128 * (m % 2):128 * (m % 2) + 128],
                        A_qfg[0:64, s, :],
                        start=True, stop=True),
                       wait=w, inc=("T" if (s == 3 or s == 7) else None))
                    if s == 3 or s == 7:
                        half = s // 4
                        schalf_T = WT("T")
                        wexp = [schalf_T]
                        if half == 0 and m >= 1:
                            wexp.append(("V", mile_sum))
                        act(lambda e, ring=ring, half=half: e.activation(
                            A_aring[:, ring, half * 1024:(half + 1) * 1024],
                            PS[:, half * 1024:(half + 1) * 1024], AF.Exp,
                            bias=stat[:, 8:9], scale=1.0,
                            accum_out=stat[:, 9 + half:10 + half]),
                            wait=wexp)
                        mile_exp_h[(m, half)] = sch.cnt["A"]
                dve(lambda e: e.tensor_scalar(stat[:, 11:12], stat[:, 9:10],
                                              stat[:, 10:11], None, op0=ALU.add),
                    wait=[("A", mile_exp_h[(m, 1)])])
                mile_sum = sch.cnt["V"]
                dve(lambda e: e.reciprocal(stat[:, 12:13], stat[:, 11:12]))
                dve(lambda e, ring=ring: e.tensor_scalar(
                    A_aring[:, ring, :], A_aring[:, ring, :], stat[:, 12:13], None,
                    op0=ALU.mult))
                mile_norm[m] = sch.cnt["V"]
                if m >= 1:
                    emit_av(m - 1)
            emit_av(15)
            av_T = WT("T")
            for rc in range(2):
                dve(lambda e, g=g, rc=rc: e.tensor_tensor(
                    x_res[:, 2 * g + rc, :], bank(4 + rc * 2, 2),
                    x_res[:, 2 * g + rc, :], op=ALU.add),
                    wait=([av_T] if rc == 0 else ()))
            evg_V = WT("V")
            OP("T", lambda e: None, wait=[evg_V])
        barrier()

    mile_hT = [sch.cnt["T"]]
    mile_y = [sch.cnt["T"]]

    def emit_ffn(w1_preloaded):
        emit_ln(list(range(NTC)), lambda tc: (F_xnT, tc * 128))
        lnf_V = WT("V")
        for tc in range(NTC):
            w = [lnf_V] if tc == 0 else []
            w.append(("V", max(0, sch.cnt["V"] - 1)))
            gbk = 6 + tc % 2
            mm_group(bank(gbk)[:, 0:8],
                     [(F_xnT[:, dmc, tc * 128:(tc + 1) * 128], wgate_sb[:, dmc, :])
                      for dmc in range(8)], wait=w)
            gm_T = WT("T")
            dve(lambda e, tc=tc, gbk=gbk: e.tensor_tensor(
                logit_sb[:, tc, :], bank(gbk)[:, 0:8], bgate_bc, op=ALU.add),
                wait=[gm_T])
        def bc16(v):
            return v.rearrange("p (a o) -> p a o", o=1).broadcast_to((128, NTC, NE))
        dve(lambda e: e.reduce_max(gmx, logit_sb, axis=AX, negate=True))
        dve(lambda e: e.tensor_tensor(gb1, logit_sb, bc16(gmx), op=ALU.add))
        gb1_V = WT("V")
        act(lambda e: e.activation(
            gexp_all.rearrange("p a b -> p (a b)"), gb1.rearrange("p a b -> p (a b)"),
            AF.Exp, bias=0.0, scale=1.0), wait=[gb1_V])
        exps_A = WT("A")
        dve(lambda e: e.tensor_reduce(gsum, gexp_all, axis=AX, op=ALU.add),
            wait=[exps_A])
        dve(lambda e: e.reciprocal(grp, gsum))
        dve(lambda e: e.tensor_scalar(gb2, gb1, 0.0, None, op0=ALU.is_equal))
        dve(lambda e: e.memset(gb3.rearrange("p a b -> p (a b)"), 0.0))
        dve(lambda e: e.tensor_copy(gb3[:, :, 1:8], gb2[:, :, 0:7]))
        dve(lambda e: e.tensor_copy(gb1, gb3))
        dve(lambda e: e.tensor_tensor(gb1[:, :, 2:8], gb3[:, :, 2:8], gb3[:, :, 0:6],
                                      op=ALU.add))
        dve(lambda e: e.tensor_copy(gb3, gb1))
        dve(lambda e: e.tensor_tensor(gb3[:, :, 4:8], gb1[:, :, 4:8], gb1[:, :, 0:4],
                                      op=ALU.add))
        dve(lambda e: e.tensor_scalar(gb1, gb3, 0.0, None, op0=ALU.is_equal))
        dve(lambda e: e.tensor_tensor(gb2, gb2, gb1, op=ALU.mult))
        dve(lambda e: e.tensor_tensor(coef_sb, gb2, bc16(grp), op=ALU.mult))
        cf_V = WT("V")
        for tc in range(NTC):
            OP("T", lambda e, tc=tc: e.transpose(bank(7)[0:8, 0:128], coef_sb[:, tc, :],
                                                 ident_f),
               wait=([cf_V] if tc == 0 else [("V", sch.cnt["V"])]), inc="T")
            dve(lambda e, tc=tc: e.tensor_copy(
                F_coefT[0:8, tc * 128:(tc + 1) * 128], bank(7)[0:8, 0:128]),
                wait=[WT("T")])
        for tc in range(NTC):
            w = [("V", sch.cnt["V"])]
            for hh in range(2):
                OP("T", lambda e, tc=tc, hh=hh: e.matmul(
                    bank(4 + hh), F_coefT[0:8, tc * 128:(tc + 1) * 128],
                    b2_sb[:, hh * 512:(hh + 1) * 512], start=True, stop=True),
                   wait=(w if hh == 0 else ()), inc="T")
            b2_T = WT("T")
            dve(lambda e, tc=tc: e.tensor_tensor(
                x_res[:, tc, :], bank(4, 2), x_res[:, tc, :], op=ALU.add), wait=[b2_T])
        b2_done_V = WT("V")

        for e_i in range(NE):
            for mg in range(4):
                if not (w1_preloaded and e_i == 0 and mg == 0):
                    for dmc in range(8):
                        dma_g(F_w1[:, dmc, :],
                              W1d[e_i][dmc * 128:(dmc + 1) * 128, mg * 1024:(mg + 1) * 1024],
                              wait=([("T", mile_hT[0])] if dmc == 0 else ()))
                w1_G = WT("G")
                ring2 = (e_i * 4 + mg) % 2
                for mc8 in range(8):
                    dma_g(F_w2[:, ring2, mc8, :],
                          W2d[e_i][mg * 1024 + mc8 * 128: mg * 1024 + (mc8 + 1) * 128, :],
                          wait=([("T", mile_y[0])] if mc8 == 0 else ()))
                w2_G = WT("G")
                for mc8 in range(8):
                    for tt in range(4):
                        rb = (mc8 * 4 + tt) % 2
                        w = []
                        if mc8 == 0 and tt == 0:
                            w = [w1_G, b2_done_V] if (e_i == 0 and mg == 0) else [w1_G]
                        w.append(ring_guard("A"))
                        mm_group(bank(rb),
                                 [(F_w1[:, dmc, mc8 * 128:(mc8 + 1) * 128],
                                   F_xnT[:, dmc, tt * 512:(tt + 1) * 512]) for dmc in range(8)],
                                 wait=w)
                        hm_T = WT("T")
                        act(lambda e, e_i=e_i, mg=mg, mc8=mc8, tt=tt, rb=rb: e.activation(
                            F_hT[:, mc8, tt * 512:(tt + 1) * 512], bank(rb), AF.Relu,
                            bias=b1_sb[:, e_i * 32 + mg * 8 + mc8: e_i * 32 + mg * 8 + mc8 + 1],
                            scale=1.0), wait=[hm_T])
                mile_hT[0] = sch.cnt["T"]
                hT_A = WT("A")
                for tc in range(NTC):
                    yb = 2 + (tc % 2) * 2
                    w = [hT_A, w2_G, ring_guard("V")]
                    for hh in range(2):
                        mm_group(bank(yb + hh),
                                 [(F_hT[:, mc8, tc * 128:(tc + 1) * 128],
                                   F_w2[:, ring2, mc8, hh * 512:(hh + 1) * 512])
                                  for mc8 in range(8)],
                                 wait=(w if hh == 0 else ()))
                    ym_T = WT("T")
                    dve(lambda e, tc=tc, yb=yb, e_i=e_i: e.scalar_tensor_tensor(
                        x_res[:, tc, :], bank(yb, 2), coef_sb[:, tc, e_i:e_i + 1],
                        x_res[:, tc, :], op0=ALU.mult, op1=ALU.add), wait=[ym_T])
                mile_y[0] = sch.cnt["T"]

    if n_sub >= 2:
        emit_ffn(w1_preloaded=True)
        barrier()

    if n_sub >= 3:
        dma_g(B_cos[:, 0, :], rcosd)
        dma_g(B_sin[:, 0, :], rsind)
        dma_g(B_wd, wdownd.rearrange("(c p) f -> p c f", p=128))
        dma_g(B_wku, wkupd.rearrange("(c p) f -> p c f", p=128))
        b1_loads = WT("G")
        for tl in range(4):
            emit_ln([4 * tl + j for j in range(4)],
                    lambda tc: (B_xn3T, (tc % 4) * 128), save_stats=True)
            ln3_V = WT("V")
            for lfc in range(4):
                rb = lfc % 2
                w = [b1_loads, ln3_V] if lfc == 0 else []
                w.append(ring_guard("A"))
                mm_group(bank(rb),
                         [(B_wd[:, dmc, lfc * 128:(lfc + 1) * 128],
                           B_xn3T[:, dmc, :]) for dmc in range(8)], wait=w)
                lm_T = WT("T")
                act(lambda e, tl=tl, lfc=lfc, rb=rb: e.activation(
                    B_latT[:, lfc, tl * 512:(tl + 1) * 512], bank(rb), AF.Copy,
                    bias=0.0, scale=1.0), wait=[lm_T])
        latT_A = WT("A")
        for tt in range(4):
            for kfc in range(8):
                rb = (tt * 8 + kfc) % 2
                w = [latT_A] if (tt == 0 and kfc == 0) else []
                w.append(ring_guard("A"))
                mm_group(bank(rb),
                         [(B_wku[:, lfc, kfc * 128:(kfc + 1) * 128],
                           B_latT[:, lfc, tt * 512:(tt + 1) * 512]) for lfc in range(4)],
                         wait=w)
                km_T = WT("T")
                act(lambda e, tt=tt, kfc=kfc, rb=rb: e.activation(
                    B_kr[:, kfc, tt * 512:(tt + 1) * 512], bank(rb), AF.Copy,
                    bias=0.0, scale=1.0), wait=[km_T])
        kT_done_A = WT("A")
        dma_g(B_wv, wvupd.rearrange("(c p) f -> p c f", p=128), wait=[WT("T")])
        wv_G = WT("G")
        for tc in range(NTC):
            for hh in range(2):
                rb = (tc * 2 + hh) % 2
                w = [wv_G, kT_done_A] if (tc == 0 and hh == 0) else []
                w.append(ring_guard("A"))
                mm_group(bank(rb),
                         [(B_latT[:, lfc, tc * 128:(tc + 1) * 128],
                           B_wv[:, lfc, hh * 512:(hh + 1) * 512]) for lfc in range(4)],
                         wait=w)
                vm_T = WT("T")
                act(lambda e, tc=tc, hh=hh, rb=rb: e.activation(
                    B_v[:, tc, hh * 512:(hh + 1) * 512], bank(rb), AF.Copy,
                    bias=0.0, scale=1.0), wait=[vm_T])
        v_done_T = WT("T")
        v_done_A = WT("A")
        for kfc in range(8):
            for tt in range(4):
                rb = 2 + (kfc * 4 + tt) % 2
                w = [kT_done_A, b1_loads] if (kfc == 0 and tt == 0) else []
                w.append(("V", max(0, sch.cnt["V"] - 2)))
                OP("T", lambda e, kfc=kfc, tt=tt, rb=rb: e.matmul(
                    bank(rb), prope_sb, B_kr[:, kfc, tt * 512:(tt + 1) * 512],
                    start=True, stop=True), wait=w, inc="T")
                sw_T = WT("T")
                dve(lambda e, kfc=kfc, tt=tt: e.tensor_tensor(
                    B_tmp[:, 0, :], B_kr[:, kfc, tt * 512:(tt + 1) * 512],
                    B_cos[:, 0, tt * 512:(tt + 1) * 512], op=ALU.mult), wait=[sw_T])
                dve(lambda e, tt=tt, rb=rb: e.tensor_tensor(
                    B_tmp[:, 1, :], bank(rb),
                    B_sin[:, 0, tt * 512:(tt + 1) * 512], op=ALU.mult))
                dve(lambda e, kfc=kfc, tt=tt: e.tensor_tensor(
                    B_kr[:, kfc, tt * 512:(tt + 1) * 512], B_tmp[:, 0, :],
                    B_tmp[:, 1, :], op=ALU.add))
        krope_V = WT("V")
        dma_g(B_wq, wqd.rearrange("(c p) f -> p c f", p=128), wait=[v_done_T])
        dma_g(B_wo, wod.rearrange("(c p) f -> p c f", p=128), wait=[krope_V])
        b2w_G = WT("G")

        for tl in range(4):
            emit_ln([4 * tl + j for j in range(4)],
                    lambda tc: (B_xn3T, (tc % 4) * 128), use_saved=True)
            ln3b_V = WT("V")
            for qfc in range(8):
                rb = qfc % 2
                w = [b2w_G, ln3b_V] if qfc == 0 else []
                w.append(ring_guard("A"))
                mm_group(bank(rb),
                         [(B_wq[:, dmc, qfc * 128:(qfc + 1) * 128],
                           B_xn3T[:, dmc, :]) for dmc in range(8)], wait=w)
                qm_T = WT("T")
                act(lambda e, qfc=qfc, rb=rb: e.activation(
                    B_qT[:, qfc, :], bank(rb), AF.Copy, bias=0.0, scale=0.125),
                    wait=[qm_T])
            qT_A = WT("A")
            for qfc in range(8):
                rb = qfc % 2
                w = [qT_A] if qfc == 0 else []
                w.append(("V", max(0, sch.cnt["V"] - 2)))
                OP("T", lambda e, qfc=qfc, rb=rb: e.matmul(
                    bank(rb), prope_sb, B_qT[:, qfc, :], start=True, stop=True),
                   wait=w, inc="T")
                swq_T = WT("T")
                dve(lambda e, qfc=qfc, tl=tl: e.tensor_tensor(
                    B_xn3T[:, 0, :], B_qT[:, qfc, :],
                    B_cos[:, 0, tl * 512:(tl + 1) * 512], op=ALU.mult), wait=[swq_T])
                dve(lambda e, rb=rb, tl=tl: e.tensor_tensor(
                    B_xn3T[:, 1, :], bank(rb),
                    B_sin[:, 0, tl * 512:(tl + 1) * 512], op=ALU.mult))
                dve(lambda e, qfc=qfc: e.tensor_tensor(
                    B_qT[:, qfc, :], B_xn3T[:, 0, :], B_xn3T[:, 1, :], op=ALU.add))
            qrope_V = WT("V")

            for pg_l in range(2):
                pg = tl * 2 + pg_l
                kmax_c = 2 * (pg + 1)
                for h in range(NH_B):
                    mile_eb = {}

                    def emit_avb(kc, h=h, kmax_c=kmax_c):
                        w_av = [("A", mile_eb[kc])]
                        if kc == 0:
                            w_av.append(("V", sch.cnt["V"]))
                        OP("T", lambda e, h=h, kc=kc, st=(kc == 0),
                           sp=(kc == kmax_c - 1):
                           e.matmul(bank(4)[0:64, 0:256],
                                    B_v[:, kc, 64 * h:64 * h + 64], B_At[:, kc % 2, :],
                                    start=st, stop=sp), wait=w_av, inc=None)
                        OP("T", lambda e, kc=kc, st=(kc == 0), sp=(kc == kmax_c - 1):
                           e.matmul(bank(5)[0:1, 0:256],
                                    ones_bf[:, 0:1], B_At[:, kc % 2, :],
                                    start=st, stop=sp), inc="T")

                    for kc in range(kmax_c):
                        rS = 2 + kc % 2
                        w = []
                        if h == 0 and kc == 0:
                            w = [qrope_V, v_done_A]
                        w.append(("A", max(0, sch.cnt["A"] - 1)))
                        OP("T", lambda e, h=h, kc=kc, rS=rS, pg_l=pg_l: e.matmul(
                            bank(rS)[:, 0:256],
                            B_kr[64 * (h % 2):64 * (h % 2) + 64, h // 2,
                                 kc * 128:(kc + 1) * 128],
                            B_qT[64 * (h % 2):64 * (h % 2) + 64, h // 2,
                                 pg_l * 256:(pg_l + 1) * 256],
                            start=True, stop=True), wait=w, inc="T")
                        s_T = WT("T")
                        act(lambda e, kc=kc, rS=rS: e.activation(
                            B_At[:, kc % 2, :], bank(rS)[:, 0:256], AF.Exp,
                            bias=stat[:, 15:16], scale=1.0), wait=[s_T])
                        mile_eb[kc] = sch.cnt["A"]
                        if kc >= 1:
                            emit_avb(kc - 1)
                    emit_avb(kmax_c - 1)
                    av_T = WT("T")
                    dve(lambda e: e.reciprocal(rrow_sb[0:1, :], bank(5)[0:1, 0:256]),
                        wait=[av_T])
                    rr_V = WT("V")
                    OP("T", lambda e: e.matmul(bank(6)[0:64, 0:256], onesf,
                                               rrow_sb[0:1, :], start=True, stop=True),
                       wait=[rr_V], inc="T")
                    rb_T = WT("T")
                    dve(lambda e: e.tensor_copy(B_rbx[0:64, :], bank(6)[0:64, 0:256]),
                        wait=[rb_T])
                    dve(lambda e, h=h: e.tensor_tensor(
                        B_aoT[64 * (h % 2):64 * (h % 2) + 64, h // 2, :],
                        bank(4)[0:64, 0:256], B_rbx[0:64, :], op=ALU.mult))
                ao_V = WT("V")
                for tc2 in range(2):
                    tc = pg * 2 + tc2
                    for hh in range(2):
                        w = [ao_V, b2w_G] if (tc2 == 0 and hh == 0) else []
                        if tc2 == 1 and hh == 0:
                            w.append(("V", sch.cnt["V"]))
                        mm_group(bank(hh),
                                 [(B_aoT[:, e1c, tc2 * 128:(tc2 + 1) * 128],
                                   B_wo[:, e1c, hh * 512:(hh + 1) * 512])
                                  for e1c in range(8)], wait=w)
                    wo_T = WT("T")
                    dve(lambda e, tc=tc: e.tensor_tensor(
                        x_res[:, tc, :], bank(0, 2), x_res[:, tc, :], op=ALU.add),
                        wait=[wo_T])
                pg_V = WT("V")
                OP("T", lambda e: None, wait=[pg_V])
        barrier()

    if n_sub >= 4:
        emit_ffn(w1_preloaded=False)
        barrier()

    final_w = sch.barrier_vals()
    for i in range(4):
        dma_s(yd_t[:, 4 * i:4 * i + 4, :], x_res[:, 4 * i:4 * i + 4, :],
              wait=(final_w if i == 0 else ()))
    if dbg:
        dmp_slab = nc.dram_tensor("dmp_slab", [128, SLAB_HALF], BF16,
                                  kind="ExternalOutput").ap()
        dmp_stat = nc.dram_tensor("dmp_stat", [128, 16 + NTC * 2], F32,
                                  kind="ExternalOutput").ap()
        for i in range(8):
            n8 = SLAB_HALF // 8
            dma_s(dmp_slab[:, i * n8:(i + 1) * n8], slab[:, i * n8:(i + 1) * n8],
                  wait=(final_w if i == 0 else ()))
        dma_s(dmp_stat[:, 0:16], stat)
        dma_s(dmp_stat[:, 16:16 + NTC * 2],
              lnst.rearrange("p a b -> p (a b)"))
    OP("S", lambda e: None, wait=[WT("S")])

    block = es.enter_context(nc.Block())

    def emit_engine(code, eng):
        last_wait = {}
        own = 0
        for engine, fn, waits, inc, inc_k in sch.ops:
            if engine != code:
                continue
            allw = list(waits)
            if code in ("V", "A") and own > 0:
                allw.append((code, own))
            for (sname, val) in allw:
                if val <= 0:
                    continue
                if last_wait.get(sname, -1) >= val:
                    continue
                eng.wait_ge(sems[sname], val)
                last_wait[sname] = val
            instr = fn(eng)
            if inc is not None:
                assert instr is not None
                instr.then_inc(sems[inc], inc_k)
                if code in ("V", "A"):
                    own += inc_k

    @block.sync
    def _(e):
        emit_engine("S", e)

    @block.gpsimd
    def _(e):
        emit_engine("G", e)

    @block.tensor
    def _(e):
        emit_engine("T", e)

    @block.scalar
    def _(e):
        emit_engine("A", e)

    @block.vector
    def _(e):
        emit_engine("V", e)

    es.close()
    return nc


def _consts():
    ident = np.eye(128, dtype=np.float32)
    onesm = np.ones((128, 128), dtype=np.float32)
    prope = np.zeros((128, 128), dtype=np.float32)
    for i in range(64):
        prope[2 * i + 1, 2 * i] = -1.0
        prope[2 * i, 2 * i + 1] = 1.0
    inv = 1.0 / (10000.0 ** (np.arange(0, 64, 2, dtype=np.float32) / 64.0))
    ang = np.outer(np.arange(T, dtype=np.float32), inv)
    cos = np.cos(ang).astype(np.float32)
    sin = np.sin(ang).astype(np.float32)
    fi = (np.arange(128) % 64) // 2
    rcos = np.ascontiguousarray(cos[:, fi].T)
    rsin = np.ascontiguousarray(sin[:, fi].T)
    return ident, onesm, prope, rcos, rsin


_prog_cache = {}


def _get_prog(n_sub=4):
    if n_sub not in _prog_cache:
        _prog_cache[n_sub] = build_program(n_sub)
    return _prog_cache[n_sub]


def kernel(L0_attn_a_x, L0_attn_a_Mq, L0_attn_a_Mk, L0_attn_a_Mv,
           L0_attn_b_wq, L0_attn_b_w_down, L0_attn_b_wk_up, L0_attn_b_wv_up,
           L0_attn_b_wo, L0_ffn_w_gate, L0_ffn_b_gate, L0_ffn_W1, L0_ffn_b1,
           L0_ffn_W2, L0_ffn_b2, _n_sub=4):
    f = lambda a: np.ascontiguousarray(np.asarray(a, dtype=np.float32))
    x = f(L0_attn_a_x)
    ident, onesm, prope, rcos, rsin = _consts()
    b1l = np.ascontiguousarray(
        f(L0_ffn_b1).reshape(NE, 32, 128).transpose(2, 0, 1).reshape(128, NE * 32))
    shared = dict(
        Mq=f(L0_attn_a_Mq), Mk=f(L0_attn_a_Mk), Mv=f(L0_attn_a_Mv),
        wq=f(L0_attn_b_wq), wdown=f(L0_attn_b_w_down), wkup=f(L0_attn_b_wk_up),
        wvup=f(L0_attn_b_wv_up), wo=f(L0_attn_b_wo),
        wgate=f(L0_ffn_w_gate),
        bgate=np.ascontiguousarray(np.tile(f(L0_ffn_b_gate).reshape(1, NE), (128, 1))),
        W1=f(L0_ffn_W1), b1l=b1l, W2=f(L0_ffn_W2), b2=f(L0_ffn_b2),
        ident=ident, onesm=onesm, prope=prope, rcos=rcos, rsin=rsin,
    )
    nc = _get_prog(_n_sub)
    in_maps = [dict(shared, x=x[c]) for c in range(8)]
    res = run_bass_kernel_spmd(nc, in_maps, core_ids=list(range(8)))
    return np.stack([res.results[c]["y"] for c in range(8)], axis=0)


# version 03
# speedup vs baseline: 1.1672x; 1.1672x over previous
import numpy as np
from contextlib import ExitStack

import concourse.bass as bass
import concourse.mybir as mybir
from concourse.bass_utils import run_bass_kernel_spmd
from concourse.alu_op_type import AluOpType as ALU

F32 = mybir.dt.float32
BF16 = mybir.dt.bfloat16
AF = mybir.ActivationFunctionType
AX = mybir.AxisListType.X

T = 2048
D = 1024
NTC = T // 128
MLP = 4096
NE = 8
NH_B = 16
C_STAB = 20.0
C_STAB_A = 10.0
EPS = 1e-5

SLAB_HALF = 65664


class Sched:
    def __init__(self):
        self.ops = []
        self.cnt = {"T": 0, "A": 0, "V": 0, "G": 0, "S": 0}

    def op(self, engine, fn, wait=(), inc=None, inc_k=1):
        if inc is not None:
            self.cnt[inc] += inc_k
        self.ops.append((engine, fn, list(wait), inc, inc_k))

    def wt(self, name):
        return (name, self.cnt[name])

    def barrier_vals(self):
        return [(k, v) for k, v in self.cnt.items() if v > 0]


def build_program(n_sub=4, dbg=False):
    nc = bass.Bass("TRN2", target_bir_lowering=False)

    xd = nc.dram_tensor("x", [T, D], F32, kind="ExternalInput").ap()
    Mqd = nc.dram_tensor("Mq", [D, 512], F32, kind="ExternalInput").ap()
    Mkd = nc.dram_tensor("Mk", [D, 512], F32, kind="ExternalInput").ap()
    Mvd = nc.dram_tensor("Mv", [D, D], F32, kind="ExternalInput").ap()
    wqd = nc.dram_tensor("wq", [D, D], F32, kind="ExternalInput").ap()
    wdownd = nc.dram_tensor("wdown", [D, 512], F32, kind="ExternalInput").ap()
    wkupd = nc.dram_tensor("wkup", [512, D], F32, kind="ExternalInput").ap()
    wvupd = nc.dram_tensor("wvup", [512, D], F32, kind="ExternalInput").ap()
    wod = nc.dram_tensor("wo", [D, D], F32, kind="ExternalInput").ap()
    wgated = nc.dram_tensor("wgate", [D, NE], F32, kind="ExternalInput").ap()
    bgated = nc.dram_tensor("bgate", [128, NE], F32, kind="ExternalInput").ap()
    W1d = nc.dram_tensor("W1", [NE, D, MLP], F32, kind="ExternalInput").ap()
    b1d = nc.dram_tensor("b1l", [128, NE * 32], F32, kind="ExternalInput").ap()
    W2d = nc.dram_tensor("W2", [NE, MLP, D], F32, kind="ExternalInput").ap()
    b2d = nc.dram_tensor("b2", [NE, D], F32, kind="ExternalInput").ap()
    identd = nc.dram_tensor("ident", [128, 128], F32, kind="ExternalInput").ap()
    onesd = nc.dram_tensor("onesm", [128, 128], F32, kind="ExternalInput").ap()
    proped = nc.dram_tensor("prope", [128, 128], F32, kind="ExternalInput").ap()
    rcosd = nc.dram_tensor("rcos", [128, T], F32, kind="ExternalInput").ap()
    rsind = nc.dram_tensor("rsin", [128, T], F32, kind="ExternalInput").ap()
    yd = nc.dram_tensor("y", [T, D], F32, kind="ExternalOutput").ap()

    xd_t = xd.rearrange("(tc p) d -> p tc d", p=128)
    yd_t = yd.rearrange("(tc p) d -> p tc d", p=128)

    es = ExitStack()

    def sb(name, shape, dtype):
        return es.enter_context(nc.sbuf_tensor(name, shape, dtype)).ap()

    x_res = sb("x_res", [128, NTC, D], F32)
    ident_bf = sb("ident_bf", [128, 128], BF16)
    ident_f = sb("ident_f", [128, 128], F32)
    ones_bf = sb("ones_bf", [128, 128], BF16)
    onesf = sb("onesf", [1, 64], F32)
    prope_sb = sb("prope_sb", [128, 128], BF16)
    b1_sb = sb("b1_sb", [128, NE * 32], F32)
    b2_sb = sb("b2_sb", [NE, D], BF16)
    wgate_sb = sb("wgate_sb", [128, NE, NE], BF16)
    bgate_bc = sb("bgate_bc", [128, NE], F32)
    xn_tile = sb("xn_tile", [128, 2, D], BF16)
    stat = sb("stat", [128, 16], F32)
    lnst = sb("lnst", [128, NTC, 2], F32)
    logit_sb = sb("logit_sb", [128, NTC, NE], F32)
    gb1 = sb("gb1", [128, NTC, NE], F32)
    gb2 = sb("gb2", [128, NTC, NE], F32)
    gb3 = sb("gb3", [128, NTC, NE], F32)
    gexp_all = sb("gexp_all", [128, NTC, NE], F32)
    gmx = sb("gmx", [128, NTC], F32)
    gsum = sb("gsum", [128, NTC], F32)
    grp = sb("grp", [128, NTC], F32)
    coef_sb = sb("coef_sb", [128, NTC, NE], F32)

    slab = sb("slab", [128, SLAB_HALF], BF16)

    def carve_bf(off_bytes, shape):
        n = int(np.prod(shape))
        v = slab[:, off_bytes // 2: off_bytes // 2 + n]
        if len(shape) == 1:
            return v
        if len(shape) == 2:
            return v.rearrange("p (a b) -> p a b", a=shape[0], b=shape[1])
        return v.rearrange("p (a b c) -> p a b c", a=shape[0], b=shape[1], c=shape[2])

    def carve_f32_flat(off_bytes, n):
        v = slab[:, off_bytes // 2: off_bytes // 2 + 2 * n].bitcast(F32)
        return v

    A_xntc = carve_bf(0, [2, 8, 128])
    A_xnTg = carve_bf(4096, [8, 512])
    A_qfg = carve_bf(12288, [8, 512])
    A_kfg = carve_bf(20480, [8, 512])
    A_aring = carve_bf(65536, [2, T])
    A_Mv = carve_bf(32768, [8, 1024])
    A_Mq = carve_bf(49152, [8, 512])
    A_Mk = carve_bf(57344, [8, 512])
    A_vf = carve_bf(98304, [16, D])
    F_xnT = carve_bf(0, [8, T])
    F_w1 = carve_bf(32768, [8, 1024])
    F_w2 = carve_bf(49152, [2, 8, 1024])
    F_hT = carve_bf(81920, [8, T])
    F_coefT = carve_bf(114688, [T])
    B_cos = carve_bf(0, [1, T])
    B_sin = carve_bf(4096, [1, T])
    B_kr = carve_bf(8192, [8, T])
    B_v = carve_bf(40960, [16, 16, 65])
    B_latT = carve_bf(74240, [4, T])
    B_wd = carve_bf(90624, [8, 512])
    B_wku = carve_bf(98816, [4, 1024])
    B_wv = carve_bf(90624, [4, 1024])
    B_xn3T = carve_bf(107520, [8, 512])
    B_tmp = carve_bf(115712, [2, 512])
    B_wq = carve_bf(74240, [8, 1024])
    B_wo = carve_bf(90624, [8, 1024])
    B_qT = carve_bf(115712, [8, 512])
    B_At = carve_bf(123904, [2, 256])
    B_aoT = carve_bf(124928, [8, 256])
    B_rbx = carve_bf(129024, [64 * 4])
    rrow_sb = carve_bf(129536, [256])

    PS = es.enter_context(nc.psum_tensor("ps_all", [128, 4096], F32)).ap()

    def bank(i, n=1):
        return PS[:, i * 512:(i + n) * 512]

    def bank_bf(i):
        return PS.bitcast(BF16)[:, i * 1024:(i + 1) * 1024]

    sems = {k: es.enter_context(nc.semaphore("sem_" + k)) for k in ["T", "A", "V", "G", "S"]}

    sch = Sched()
    OP, WT = sch.op, sch.wt

    def dma_g(dst, src, wait=()):
        OP("G", lambda e, d=dst, s=src: e.dma_start(out=d, in_=s), wait=wait, inc="G", inc_k=16)

    def dma_s(dst, src, wait=()):
        OP("S", lambda e, d=dst, s=src: e.dma_start(out=d, in_=s), wait=wait, inc="S", inc_k=16)

    def mm_group(out, pairs, wait=()):
        n = len(pairs)
        for i, (l, r) in enumerate(pairs):
            OP("T", lambda e, o=out, l=l, r=r, st=(i == 0), sp=(i == n - 1):
               e.matmul(o, l, r, start=st, stop=sp),
               wait=(wait if i == 0 else ()), inc=("T" if i == n - 1 else None))

    def act(fn, wait=()):
        OP("A", fn, wait=wait, inc="A")

    def dve(fn, wait=()):
        OP("V", fn, wait=wait, inc="V")

    def barrier():
        vals = sch.barrier_vals()
        for eng in ["T", "A", "V", "G", "S"]:
            w = [(k, v) for k, v in vals if k != eng]
            OP(eng, lambda e: None, wait=w)

    def ring_guard(sem):
        return (sem, max(0, sch.cnt[sem] - 1))

    for i in range(4):
        dma_s(x_res[:, 4 * i:4 * i + 4, :], xd_t[:, 4 * i:4 * i + 4, :])
    x_loaded = WT("S")
    dve(lambda e: e.memset(stat[:, 14:15], EPS))
    dve(lambda e: e.memset(stat[:, 15:16], -C_STAB))
    dve(lambda e: e.memset(stat[:, 8:9], -C_STAB_A))
    dma_g(ident_bf, identd)
    dma_g(ident_f, identd)
    dma_g(ones_bf, onesd)
    dma_g(onesf, onesd[0:1, 0:64])
    dma_g(prope_sb, proped)
    dma_g(b1_sb, b1d)
    dma_g(b2_sb, b2d)
    dma_g(wgate_sb, wgated.rearrange("(c p) e -> p c e", p=128))
    dma_g(bgate_bc, bgated)
    dma_g(A_Mq, Mqd.rearrange("(c p) f -> p c f", p=128))
    dma_g(A_Mk, Mkd.rearrange("(c p) f -> p c f", p=128))
    dma_g(A_Mv, Mvd.rearrange("(c p) f -> p c f", p=128))
    w_loads0 = WT("G")

    def emit_ln(tc_list, dst_of, save_stats=False, use_saved=False, after_tile=None):
        for j, tc in enumerate(tc_list):
            ring = tc % 2
            if use_saved:
                rstd = lnst[:, tc, 0:1]
                nmr = lnst[:, tc, 1:2]
            else:
                rstd = stat[:, 0:1]
                nmr = stat[:, 1:2]
                s1 = stat[:, 2:3]
                sq = stat[:, 3:4]
                m = stat[:, 4:5]
                var = stat[:, 5:6]
                dve(lambda e, tc=tc, s1=s1: e.reduce_sum(s1, x_res[:, tc, :], axis=AX),
                    wait=([x_loaded] if j == 0 else ()))
                dve(lambda e, tc=tc, ring=ring, sq=sq: e.scalar_tensor_tensor(
                    xn_tile[:, ring, :], x_res[:, tc, :], 1.0, x_res[:, tc, :],
                    op0=ALU.bypass, op1=ALU.mult, accum_out=sq))
                dve(lambda e, s1=s1, m=m: e.tensor_scalar(m, s1, 1.0 / D, None, op0=ALU.mult))
                dve(lambda e, m=m, var=var: e.tensor_scalar(var, m, m[:, 0:1], None,
                                                            op0=ALU.mult))
                dve(lambda e, sq=sq, var=var: e.scalar_tensor_tensor(
                    var, sq, 1.0 / D, var, op0=ALU.mult, op1=ALU.subtract))
                act(lambda e, var=var: e.activation(stat[:, 6:7], var, AF.Ln,
                                                    bias=stat[:, 14:15], scale=1.0),
                    wait=[WT("V")])
                act(lambda e, rstd=rstd: e.activation(rstd, stat[:, 6:7], AF.Exp, bias=0.0, scale=-0.5))
                dve(lambda e, m=m, rstd=rstd, nmr=nmr: e.tensor_scalar(
                    nmr, m, rstd[:, 0:1], -1.0, op0=ALU.mult, op1=ALU.mult), wait=[WT("A")])
                if save_stats:
                    dve(lambda e, tc=tc, rstd=rstd: e.tensor_copy(lnst[:, tc, 0:1], rstd))
                    dve(lambda e, tc=tc, nmr=nmr: e.tensor_copy(lnst[:, tc, 1:2], nmr))
            act(lambda e, tc=tc, ring=ring, rstd=rstd, nmr=nmr: e.activation(
                xn_tile[:, ring, :], x_res[:, tc, :], AF.Identity,
                bias=nmr[:, 0:1], scale=rstd[:, 0:1]),
                wait=[WT("V")])
            a_apply = WT("A")
            dst, co = dst_of(tc)
            for dmc in range(8):
                OP("T", lambda e, ring=ring, dmc=dmc: e.transpose(
                    bank_bf(7)[:, dmc * 128:(dmc + 1) * 128],
                    xn_tile[:, ring, dmc * 128:(dmc + 1) * 128], ident_bf),
                   wait=([a_apply, WT("V")] if dmc == 0 else ()),
                   inc=("T" if dmc == 7 else None))
            tdone = WT("T")
            dve(lambda e, co=co, dst=dst: e.tensor_copy(
                dst[:, :, co:co + 128],
                bank_bf(7).rearrange("p (c f) -> p c f", c=8)),
                wait=[tdone])
            if after_tile is not None:
                after_tile(tc)

    def emit_vf(tc):
        ring = tc % 2
        cp_V = WT("V")
        for half in range(2):
            rb = (tc * 2 + half) % 2
            w = [cp_V, w_loads0] if half == 0 else []
            w.append(ring_guard("A"))
            mm_group(bank(rb),
                     [(A_xntc[:, ring, dmc, :], A_Mv[:, dmc, half * 512:(half + 1) * 512])
                      for dmc in range(8)], wait=w)
            mmil = WT("T")
            act(lambda e, tc=tc, half=half, rb=rb: e.activation(
                A_vf[:, tc, half * 512:(half + 1) * 512], bank(rb), AF.Copy,
                bias=0.0, scale=1.0), wait=[mmil])

    emit_ln(list(range(NTC)), lambda tc: (A_xntc[:, tc % 2], 0),
            save_stats=True, after_tile=emit_vf)
    p1_done_A = WT("A")
    p1_done_T = WT("T")

    if n_sub >= 1:
        if n_sub >= 2:
            for dmc in range(8):
                dma_g(F_w1[:, dmc, :], W1d[0][dmc * 128:(dmc + 1) * 128, 0:1024],
                      wait=([p1_done_T] if dmc == 0 else ()))

        mile_sum = 0
        for gp in range(4):
            emit_ln([4 * gp + j for j in range(4)],
                    lambda tc: (A_xnTg, (tc % 4) * 128), use_saved=True)
            lng_V = WT("V")
            for which, Msb, dstg, scale in ((0, A_Mq, A_qfg, 0.125),
                                            (1, A_Mk, A_kfg, 1.0)):
                for s in range(8):
                    rb = s % 2
                    w = [lng_V] if (which == 0 and s == 0) else []
                    w.append(ring_guard("A"))
                    mm_group(bank(rb)[0:64, 0:512],
                             [(Msb[:, dmc, 64 * s:64 * s + 64], A_xnTg[:, dmc, :])
                              for dmc in range(8)], wait=w)
                    mmil = WT("T")
                    act(lambda e, s=s, rb=rb, dstg=dstg, scale=scale: e.activation(
                        dstg[0:64, s, :], bank(rb)[0:64, 0:512], AF.Copy,
                        bias=0.0, scale=scale), wait=[mmil])
            qk_A = WT("A")
            for gl in range(2):
              g = 2 * gp + gl
              if True:
                mile_exp_h = {}
                mile_norm = {}

              def emit_av(j):
                  jr = j % 2
                  for rc in range(2):
                      for h2 in range(8):
                          bi = 4 + rc * 2 + h2 // 4
                          off = (h2 % 4) * 128
                          st = (j == 0) and (h2 % 4 == 0)
                          sp = (j == 15) and (h2 % 4 == 3)
                          w = [("V", mile_norm[j])] if (rc == 0 and h2 == 0) else ()
                          OP("T", lambda e, j=j, rc=rc, h2=h2, bi=bi, off=off, st=st,
                             sp=sp, jr=jr: e.matmul(
                              bank(bi)[:, off:off + 128],
                              A_aring[:, jr, 256 * h2 + 128 * rc: 256 * h2 + 128 * rc + 128],
                              A_vf[:, 2 * h2 + (j % 2), 128 * (j // 2):128 * (j // 2) + 128],
                              start=st, stop=sp),
                             wait=w, inc=("T" if (rc == 1 and h2 == 7) else None))

              for m in range(16):
                  ring = m % 2
                  tS = m // 2
                  for s in range(8):
                      bi = s // 2
                      off = (s % 2) * 256
                      w = []
                      if s == 0 or s == 4:
                          half = s // 4
                          if m == 0:
                              w.append(("A", sch.cnt["A"]))
                          else:
                              w.append(("A", mile_exp_h[(m - 1, half)]))
                      OP("T", lambda e, m=m, s=s, bi=bi, off=off, tS=tS, gl=gl: e.matmul(
                          bank(bi)[:, off:off + 256],
                          A_kfg[0:64, tS, 256 * gl + 128 * (m % 2):256 * gl + 128 * (m % 2) + 128],
                          A_qfg[0:64, s, 256 * gl:256 * gl + 256],
                          start=True, stop=True),
                         wait=w, inc=("T" if (s == 3 or s == 7) else None))
                      if s == 3 or s == 7:
                          half = s // 4
                          schalf_T = WT("T")
                          wexp = [schalf_T]
                          if half == 0 and m >= 1:
                              wexp.append(("V", mile_sum))
                          act(lambda e, ring=ring, half=half: e.activation(
                              A_aring[:, ring, half * 1024:(half + 1) * 1024],
                              PS[:, half * 1024:(half + 1) * 1024], AF.Exp,
                              bias=stat[:, 8:9], scale=1.0,
                              accum_out=stat[:, 9 + half:10 + half]),
                              wait=wexp)
                          mile_exp_h[(m, half)] = sch.cnt["A"]
                  dve(lambda e: e.tensor_scalar(stat[:, 11:12], stat[:, 9:10],
                                                stat[:, 10:11], None, op0=ALU.add),
                      wait=[("A", mile_exp_h[(m, 1)])])
                  mile_sum = sch.cnt["V"]
                  dve(lambda e: e.reciprocal(stat[:, 12:13], stat[:, 11:12]))
                  dve(lambda e, ring=ring: e.tensor_scalar(
                      A_aring[:, ring, :], A_aring[:, ring, :], stat[:, 12:13], None,
                      op0=ALU.mult))
                  mile_norm[m] = sch.cnt["V"]
                  if m >= 1:
                      emit_av(m - 1)
              emit_av(15)
              av_T = WT("T")
              for rc in range(2):
                  dve(lambda e, g=g, rc=rc: e.tensor_tensor(
                      x_res[:, 2 * g + rc, :], bank(4 + rc * 2, 2),
                      x_res[:, 2 * g + rc, :], op=ALU.add),
                      wait=([av_T] if rc == 0 else ()))
              evg_V = WT("V")
              OP("T", lambda e: None, wait=[evg_V])
        barrier()

    mile_hT = [sch.cnt["T"]]
    mile_y = [sch.cnt["T"]]

    def emit_ffn(w1_preloaded):
        emit_ln(list(range(NTC)), lambda tc: (F_xnT, tc * 128))
        lnf_V = WT("V")
        for tc in range(NTC):
            w = [lnf_V] if tc == 0 else []
            w.append(("V", max(0, sch.cnt["V"] - 1)))
            gbk = 6 + tc % 2
            mm_group(bank(gbk)[:, 0:8],
                     [(F_xnT[:, dmc, tc * 128:(tc + 1) * 128], wgate_sb[:, dmc, :])
                      for dmc in range(8)], wait=w)
            gm_T = WT("T")
            dve(lambda e, tc=tc, gbk=gbk: e.tensor_tensor(
                logit_sb[:, tc, :], bank(gbk)[:, 0:8], bgate_bc, op=ALU.add),
                wait=[gm_T])
        def bc16(v):
            return v.rearrange("p (a o) -> p a o", o=1).broadcast_to((128, NTC, NE))
        dve(lambda e: e.reduce_max(gmx, logit_sb, axis=AX, negate=True))
        dve(lambda e: e.tensor_tensor(gb1, logit_sb, bc16(gmx), op=ALU.add))
        gb1_V = WT("V")
        act(lambda e: e.activation(
            gexp_all.rearrange("p a b -> p (a b)"), gb1.rearrange("p a b -> p (a b)"),
            AF.Exp, bias=0.0, scale=1.0), wait=[gb1_V])
        exps_A = WT("A")
        dve(lambda e: e.tensor_reduce(gsum, gexp_all, axis=AX, op=ALU.add),
            wait=[exps_A])
        dve(lambda e: e.reciprocal(grp, gsum))
        dve(lambda e: e.tensor_scalar(gb2, gb1, 0.0, None, op0=ALU.is_equal))
        dve(lambda e: e.memset(gb3.rearrange("p a b -> p (a b)"), 0.0))
        dve(lambda e: e.tensor_copy(gb3[:, :, 1:8], gb2[:, :, 0:7]))
        dve(lambda e: e.tensor_copy(gb1, gb3))
        dve(lambda e: e.tensor_tensor(gb1[:, :, 2:8], gb3[:, :, 2:8], gb3[:, :, 0:6],
                                      op=ALU.add))
        dve(lambda e: e.tensor_copy(gb3, gb1))
        dve(lambda e: e.tensor_tensor(gb3[:, :, 4:8], gb1[:, :, 4:8], gb1[:, :, 0:4],
                                      op=ALU.add))
        dve(lambda e: e.tensor_scalar(gb1, gb3, 0.0, None, op0=ALU.is_equal))
        dve(lambda e: e.tensor_tensor(gb2, gb2, gb1, op=ALU.mult))
        dve(lambda e: e.tensor_tensor(coef_sb, gb2, bc16(grp), op=ALU.mult))
        cf_V = WT("V")
        for tc in range(NTC):
            OP("T", lambda e, tc=tc: e.transpose(bank(7)[0:8, 0:128], coef_sb[:, tc, :],
                                                 ident_f),
               wait=([cf_V] if tc == 0 else [("V", sch.cnt["V"])]), inc="T")
            dve(lambda e, tc=tc: e.tensor_copy(
                F_coefT[0:8, tc * 128:(tc + 1) * 128], bank(7)[0:8, 0:128]),
                wait=[WT("T")])
        for tc in range(NTC):
            w = [("V", sch.cnt["V"])]
            for hh in range(2):
                OP("T", lambda e, tc=tc, hh=hh: e.matmul(
                    bank(4 + hh), F_coefT[0:8, tc * 128:(tc + 1) * 128],
                    b2_sb[:, hh * 512:(hh + 1) * 512], start=True, stop=True),
                   wait=(w if hh == 0 else ()), inc="T")
            b2_T = WT("T")
            dve(lambda e, tc=tc: e.tensor_tensor(
                x_res[:, tc, :], bank(4, 2), x_res[:, tc, :], op=ALU.add), wait=[b2_T])
        b2_done_V = WT("V")

        for e_i in range(NE):
            for mg in range(4):
                if not (w1_preloaded and e_i == 0 and mg == 0):
                    for dmc in range(8):
                        dma_g(F_w1[:, dmc, :],
                              W1d[e_i][dmc * 128:(dmc + 1) * 128, mg * 1024:(mg + 1) * 1024],
                              wait=([("T", mile_hT[0])] if dmc == 0 else ()))
                w1_G = WT("G")
                ring2 = (e_i * 4 + mg) % 2
                for mc8 in range(8):
                    dma_g(F_w2[:, ring2, mc8, :],
                          W2d[e_i][mg * 1024 + mc8 * 128: mg * 1024 + (mc8 + 1) * 128, :],
                          wait=([("T", mile_y[0])] if mc8 == 0 else ()))
                w2_G = WT("G")
                for mc8 in range(8):
                    for tt in range(4):
                        rb = (mc8 * 4 + tt) % 2
                        w = []
                        if mc8 == 0 and tt == 0:
                            w = [w1_G, b2_done_V] if (e_i == 0 and mg == 0) else [w1_G]
                        w.append(ring_guard("A"))
                        mm_group(bank(rb),
                                 [(F_w1[:, dmc, mc8 * 128:(mc8 + 1) * 128],
                                   F_xnT[:, dmc, tt * 512:(tt + 1) * 512]) for dmc in range(8)],
                                 wait=w)
                        hm_T = WT("T")
                        act(lambda e, e_i=e_i, mg=mg, mc8=mc8, tt=tt, rb=rb: e.activation(
                            F_hT[:, mc8, tt * 512:(tt + 1) * 512], bank(rb), AF.Relu,
                            bias=b1_sb[:, e_i * 32 + mg * 8 + mc8: e_i * 32 + mg * 8 + mc8 + 1],
                            scale=1.0), wait=[hm_T])
                mile_hT[0] = sch.cnt["T"]
                hT_A = WT("A")
                for tc in range(NTC):
                    yb = 2 + (tc % 2) * 2
                    w = [hT_A, w2_G, ring_guard("V")]
                    for hh in range(2):
                        mm_group(bank(yb + hh),
                                 [(F_hT[:, mc8, tc * 128:(tc + 1) * 128],
                                   F_w2[:, ring2, mc8, hh * 512:(hh + 1) * 512])
                                  for mc8 in range(8)],
                                 wait=(w if hh == 0 else ()))
                    ym_T = WT("T")
                    dve(lambda e, tc=tc, yb=yb, e_i=e_i: e.scalar_tensor_tensor(
                        x_res[:, tc, :], bank(yb, 2), coef_sb[:, tc, e_i:e_i + 1],
                        x_res[:, tc, :], op0=ALU.mult, op1=ALU.add), wait=[ym_T])
                mile_y[0] = sch.cnt["T"]

    if n_sub >= 2:
        emit_ffn(w1_preloaded=True)
        barrier()

    if n_sub >= 3:
        dma_g(B_cos[:, 0, :], rcosd)
        dma_g(B_sin[:, 0, :], rsind)
        dma_g(B_wd, wdownd.rearrange("(c p) f -> p c f", p=128))
        dma_g(B_wku, wkupd.rearrange("(c p) f -> p c f", p=128))
        b1_loads = WT("G")
        for tl in range(4):
            emit_ln([4 * tl + j for j in range(4)],
                    lambda tc: (B_xn3T, (tc % 4) * 128), save_stats=True)
            ln3_V = WT("V")
            for lfc in range(4):
                rb = lfc % 2
                w = [b1_loads, ln3_V] if lfc == 0 else []
                w.append(ring_guard("A"))
                mm_group(bank(rb),
                         [(B_wd[:, dmc, lfc * 128:(lfc + 1) * 128],
                           B_xn3T[:, dmc, :]) for dmc in range(8)], wait=w)
                lm_T = WT("T")
                act(lambda e, tl=tl, lfc=lfc, rb=rb: e.activation(
                    B_latT[:, lfc, tl * 512:(tl + 1) * 512], bank(rb), AF.Copy,
                    bias=0.0, scale=1.0), wait=[lm_T])
        latT_A = WT("A")
        for tt in range(4):
            for kfc in range(8):
                rb = (tt * 8 + kfc) % 2
                w = [latT_A] if (tt == 0 and kfc == 0) else []
                w.append(ring_guard("A"))
                mm_group(bank(rb),
                         [(B_wku[:, lfc, kfc * 128:(kfc + 1) * 128],
                           B_latT[:, lfc, tt * 512:(tt + 1) * 512]) for lfc in range(4)],
                         wait=w)
                km_T = WT("T")
                act(lambda e, tt=tt, kfc=kfc, rb=rb: e.activation(
                    B_kr[:, kfc, tt * 512:(tt + 1) * 512], bank(rb), AF.Copy,
                    bias=0.0, scale=1.0), wait=[km_T])
        kT_done_A = WT("A")
        dma_g(B_wv, wvupd.rearrange("(c p) f -> p c f", p=128), wait=[WT("T")])
        wv_G = WT("G")
        for tc in range(NTC):
            for hh in range(2):
                rb = (tc * 2 + hh) % 2
                w = [wv_G, kT_done_A] if (tc == 0 and hh == 0) else []
                w.append(ring_guard("A"))
                mm_group(bank(rb),
                         [(B_latT[:, lfc, tc * 128:(tc + 1) * 128],
                           B_wv[:, lfc, hh * 512:(hh + 1) * 512]) for lfc in range(4)],
                         wait=w)
                vm_T = WT("T")
                act(lambda e, tc=tc, hh=hh, rb=rb: e.activation(
                    B_v[:, tc, 8 * hh:8 * hh + 8, 0:64],
                    bank(rb), AF.Copy, bias=0.0, scale=1.0), wait=[vm_T])
        dve(lambda e: e.memset(B_v[:, :, :, 64:65], 1.0))
        v_done_T = WT("T")
        v_done_A = WT("A")
        v_ones_V = WT("V")
        for kfc in range(8):
            for tt in range(4):
                rb = 2 + (kfc * 4 + tt) % 2
                w = [kT_done_A, b1_loads] if (kfc == 0 and tt == 0) else []
                w.append(("V", max(0, sch.cnt["V"] - 2)))
                OP("T", lambda e, kfc=kfc, tt=tt, rb=rb: e.matmul(
                    bank(rb), prope_sb, B_kr[:, kfc, tt * 512:(tt + 1) * 512],
                    start=True, stop=True), wait=w, inc="T")
                sw_T = WT("T")
                dve(lambda e, kfc=kfc, tt=tt: e.tensor_tensor(
                    B_tmp[:, 0, :], B_kr[:, kfc, tt * 512:(tt + 1) * 512],
                    B_cos[:, 0, tt * 512:(tt + 1) * 512], op=ALU.mult), wait=[sw_T])
                dve(lambda e, tt=tt, rb=rb: e.tensor_tensor(
                    B_tmp[:, 1, :], bank(rb),
                    B_sin[:, 0, tt * 512:(tt + 1) * 512], op=ALU.mult))
                dve(lambda e, kfc=kfc, tt=tt: e.tensor_tensor(
                    B_kr[:, kfc, tt * 512:(tt + 1) * 512], B_tmp[:, 0, :],
                    B_tmp[:, 1, :], op=ALU.add))
        krope_V = WT("V")
        dma_g(B_wq, wqd.rearrange("(c p) f -> p c f", p=128), wait=[v_done_T])
        dma_g(B_wo, wod.rearrange("(c p) f -> p c f", p=128), wait=[krope_V])
        b2w_G = WT("G")

        for tl in range(4):
            emit_ln([4 * tl + j for j in range(4)],
                    lambda tc: (B_xn3T, (tc % 4) * 128), use_saved=True)
            ln3b_V = WT("V")
            for qfc in range(8):
                rb = qfc % 2
                w = [b2w_G, ln3b_V] if qfc == 0 else []
                w.append(ring_guard("A"))
                mm_group(bank(rb),
                         [(B_wq[:, dmc, qfc * 128:(qfc + 1) * 128],
                           B_xn3T[:, dmc, :]) for dmc in range(8)], wait=w)
                qm_T = WT("T")
                act(lambda e, qfc=qfc, rb=rb: e.activation(
                    B_qT[:, qfc, :], bank(rb), AF.Copy, bias=0.0, scale=0.125),
                    wait=[qm_T])
            qT_A = WT("A")
            for qfc in range(8):
                rb = qfc % 2
                w = [qT_A] if qfc == 0 else []
                w.append(("V", max(0, sch.cnt["V"] - 2)))
                OP("T", lambda e, qfc=qfc, rb=rb: e.matmul(
                    bank(rb), prope_sb, B_qT[:, qfc, :], start=True, stop=True),
                   wait=w, inc="T")
                swq_T = WT("T")
                dve(lambda e, qfc=qfc, tl=tl: e.tensor_tensor(
                    B_xn3T[:, 0, :], B_qT[:, qfc, :],
                    B_cos[:, 0, tl * 512:(tl + 1) * 512], op=ALU.mult), wait=[swq_T])
                dve(lambda e, rb=rb, tl=tl: e.tensor_tensor(
                    B_xn3T[:, 1, :], bank(rb),
                    B_sin[:, 0, tl * 512:(tl + 1) * 512], op=ALU.mult))
                dve(lambda e, qfc=qfc: e.tensor_tensor(
                    B_qT[:, qfc, :], B_xn3T[:, 0, :], B_xn3T[:, 1, :], op=ALU.add))
            qrope_V = WT("V")

            for pg_l in range(2):
                pg = tl * 2 + pg_l
                kmax_c = 2 * (pg + 1)
                for h in range(NH_B):
                    mile_eb = {}

                    def emit_avb(kc, h=h, kmax_c=kmax_c):
                        w_av = [("A", mile_eb[kc])]
                        if kc == 0:
                            w_av.append(("V", sch.cnt["V"]))
                        OP("T", lambda e, h=h, kc=kc, st=(kc == 0),
                           sp=(kc == kmax_c - 1):
                           e.matmul(bank(4)[0:65, 0:256],
                                    B_v[:, kc, h, :], B_At[:, kc % 2, :],
                                    start=st, stop=sp), wait=w_av, inc="T")

                    for kc in range(kmax_c):
                        rS = 2 + kc % 2
                        w = []
                        if h == 0 and kc == 0:
                            w = [qrope_V, v_done_A]
                        w.append(("A", max(0, sch.cnt["A"] - 1)))
                        OP("T", lambda e, h=h, kc=kc, rS=rS, pg_l=pg_l: e.matmul(
                            bank(rS)[:, 0:256],
                            B_kr[64 * (h % 2):64 * (h % 2) + 64, h // 2,
                                 kc * 128:(kc + 1) * 128],
                            B_qT[64 * (h % 2):64 * (h % 2) + 64, h // 2,
                                 pg_l * 256:(pg_l + 1) * 256],
                            start=True, stop=True), wait=w, inc="T")
                        s_T = WT("T")
                        act(lambda e, kc=kc, rS=rS: e.activation(
                            B_At[:, kc % 2, :], bank(rS)[:, 0:256], AF.Exp,
                            bias=stat[:, 15:16], scale=1.0), wait=[s_T])
                        mile_eb[kc] = sch.cnt["A"]
                        if kc >= 1:
                            emit_avb(kc - 1)
                    emit_avb(kmax_c - 1)
                    av_T = WT("T")

                    def _recip_lp(e):
                        with nc.allow_low_precision(reason="bf16 softmax normaliser"):
                            return e.reciprocal(rrow_sb[0:1, 0:256],
                                                bank(4)[64:65, 0:256])
                    dve(_recip_lp, wait=[av_T])
                    rr_V = WT("V")
                    OP("T", lambda e: e.matmul(bank(6)[0:64, 0:256], ones_bf[0:1, 0:64],
                                               rrow_sb[0:1, 0:256], start=True, stop=True),
                       wait=[rr_V], inc="T")
                    rb_T = WT("T")
                    dve(lambda e: e.tensor_copy(B_rbx[0:64, 0:256],
                                                bank(6)[0:64, 0:256]),
                        wait=[rb_T])
                    dve(lambda e, h=h: e.tensor_tensor(
                        B_aoT[64 * (h % 2):64 * (h % 2) + 64, h // 2, :],
                        bank(4)[0:64, 0:256], B_rbx[0:64, 0:256], op=ALU.mult))
                ao_V = WT("V")
                for tc2 in range(2):
                    tc = pg * 2 + tc2
                    for hh in range(2):
                        w = [ao_V, b2w_G] if (tc2 == 0 and hh == 0) else []
                        if tc2 == 1 and hh == 0:
                            w.append(("V", sch.cnt["V"]))
                        mm_group(bank(hh),
                                 [(B_aoT[:, e1c, tc2 * 128:(tc2 + 1) * 128],
                                   B_wo[:, e1c, hh * 512:(hh + 1) * 512])
                                  for e1c in range(8)], wait=w)
                    wo_T = WT("T")
                    dve(lambda e, tc=tc: e.tensor_tensor(
                        x_res[:, tc, :], bank(0, 2), x_res[:, tc, :], op=ALU.add),
                        wait=[wo_T])
                pg_V = WT("V")
                OP("T", lambda e: None, wait=[pg_V])
        barrier()

    if n_sub >= 4:
        emit_ffn(w1_preloaded=False)
        barrier()

    final_w = sch.barrier_vals()
    for i in range(4):
        dma_s(yd_t[:, 4 * i:4 * i + 4, :], x_res[:, 4 * i:4 * i + 4, :],
              wait=(final_w if i == 0 else ()))
    if dbg:
        dmp_slab = nc.dram_tensor("dmp_slab", [128, SLAB_HALF], BF16,
                                  kind="ExternalOutput").ap()
        dmp_stat = nc.dram_tensor("dmp_stat", [128, 16 + NTC * 2], F32,
                                  kind="ExternalOutput").ap()
        for i in range(8):
            n8 = SLAB_HALF // 8
            dma_s(dmp_slab[:, i * n8:(i + 1) * n8], slab[:, i * n8:(i + 1) * n8],
                  wait=(final_w if i == 0 else ()))
        dma_s(dmp_stat[:, 0:16], stat)
        dma_s(dmp_stat[:, 16:16 + NTC * 2],
              lnst.rearrange("p a b -> p (a b)"))
    OP("S", lambda e: None, wait=[WT("S")])

    block = es.enter_context(nc.Block())

    def emit_engine(code, eng):
        last_wait = {}
        own = 0
        for engine, fn, waits, inc, inc_k in sch.ops:
            if engine != code:
                continue
            allw = list(waits)
            if code in ("V", "A") and own > 0:
                allw.append((code, own))
            for (sname, val) in allw:
                if val <= 0:
                    continue
                if last_wait.get(sname, -1) >= val:
                    continue
                eng.wait_ge(sems[sname], val)
                last_wait[sname] = val
            instr = fn(eng)
            if inc is not None:
                assert instr is not None
                instr.then_inc(sems[inc], inc_k)
                if code in ("V", "A"):
                    own += inc_k

    @block.sync
    def _(e):
        emit_engine("S", e)

    @block.gpsimd
    def _(e):
        emit_engine("G", e)

    @block.tensor
    def _(e):
        emit_engine("T", e)

    @block.scalar
    def _(e):
        emit_engine("A", e)

    @block.vector
    def _(e):
        emit_engine("V", e)

    es.close()
    return nc


def _consts():
    ident = np.eye(128, dtype=np.float32)
    onesm = np.ones((128, 128), dtype=np.float32)
    prope = np.zeros((128, 128), dtype=np.float32)
    for i in range(64):
        prope[2 * i + 1, 2 * i] = -1.0
        prope[2 * i, 2 * i + 1] = 1.0
    inv = 1.0 / (10000.0 ** (np.arange(0, 64, 2, dtype=np.float32) / 64.0))
    ang = np.outer(np.arange(T, dtype=np.float32), inv)
    cos = np.cos(ang).astype(np.float32)
    sin = np.sin(ang).astype(np.float32)
    fi = (np.arange(128) % 64) // 2
    rcos = np.ascontiguousarray(cos[:, fi].T)
    rsin = np.ascontiguousarray(sin[:, fi].T)
    return ident, onesm, prope, rcos, rsin


_prog_cache = {}


def _get_prog(n_sub=4):
    if n_sub not in _prog_cache:
        _prog_cache[n_sub] = build_program(n_sub)
    return _prog_cache[n_sub]


def kernel(L0_attn_a_x, L0_attn_a_Mq, L0_attn_a_Mk, L0_attn_a_Mv,
           L0_attn_b_wq, L0_attn_b_w_down, L0_attn_b_wk_up, L0_attn_b_wv_up,
           L0_attn_b_wo, L0_ffn_w_gate, L0_ffn_b_gate, L0_ffn_W1, L0_ffn_b1,
           L0_ffn_W2, L0_ffn_b2, _n_sub=4):
    f = lambda a: np.ascontiguousarray(np.asarray(a, dtype=np.float32))
    x = f(L0_attn_a_x)
    ident, onesm, prope, rcos, rsin = _consts()
    b1l = np.ascontiguousarray(
        f(L0_ffn_b1).reshape(NE, 32, 128).transpose(2, 0, 1).reshape(128, NE * 32))
    shared = dict(
        Mq=f(L0_attn_a_Mq), Mk=f(L0_attn_a_Mk), Mv=f(L0_attn_a_Mv),
        wq=f(L0_attn_b_wq), wdown=f(L0_attn_b_w_down), wkup=f(L0_attn_b_wk_up),
        wvup=f(L0_attn_b_wv_up), wo=f(L0_attn_b_wo),
        wgate=f(L0_ffn_w_gate),
        bgate=np.ascontiguousarray(np.tile(f(L0_ffn_b_gate).reshape(1, NE), (128, 1))),
        W1=f(L0_ffn_W1), b1l=b1l, W2=f(L0_ffn_W2), b2=f(L0_ffn_b2),
        ident=ident, onesm=onesm, prope=prope, rcos=rcos, rsin=rsin,
    )
    nc = _get_prog(_n_sub)
    in_maps = [dict(shared, x=x[c]) for c in range(8)]
    res = run_bass_kernel_spmd(nc, in_maps, core_ids=list(range(8)))
    return np.stack([res.results[c]["y"] for c in range(8)], axis=0)


# version 06
# speedup vs baseline: 1.5972x; 1.3685x over previous
import numpy as np
from contextlib import ExitStack

import concourse.bass as bass
import concourse.mybir as mybir
from concourse.bass_utils import run_bass_kernel_spmd
from concourse.alu_op_type import AluOpType as ALU

F32 = mybir.dt.float32
BF16 = mybir.dt.bfloat16
AF = mybir.ActivationFunctionType
AX = mybir.AxisListType.X

T = 2048
D = 1024
NTC = T // 128
MLP = 4096
NE = 8
NH_B = 16
C_STAB = 20.0
C_STAB_A = 10.0
EPS = 1e-5
CAP = 384

SLAB_HALF = 65664


class Sched:
    def __init__(self):
        self.ops = []
        self.cnt = {"T": 0, "A": 0, "V": 0, "G": 0, "S": 0}

    def op(self, engine, fn, wait=(), inc=None, inc_k=1):
        if inc is not None:
            self.cnt[inc] += inc_k
        self.ops.append((engine, fn, list(wait), inc, inc_k))

    def wt(self, name):
        return (name, self.cnt[name])

    def barrier_vals(self):
        return [(k, v) for k, v in self.cnt.items() if v > 0]


def build_program(n_sub=4, dbg=False):
    nc = bass.Bass("TRN2", target_bir_lowering=False)

    xd = nc.dram_tensor("x", [T, D], F32, kind="ExternalInput").ap()
    Mqd = nc.dram_tensor("Mq", [D, 512], F32, kind="ExternalInput").ap()
    Mkd = nc.dram_tensor("Mk", [D, 512], F32, kind="ExternalInput").ap()
    Mvd = nc.dram_tensor("Mv", [D, D], F32, kind="ExternalInput").ap()
    wqd = nc.dram_tensor("wq", [D, D], F32, kind="ExternalInput").ap()
    wdownd = nc.dram_tensor("wdown", [D, 512], F32, kind="ExternalInput").ap()
    wkupd = nc.dram_tensor("wkup", [512, D], F32, kind="ExternalInput").ap()
    wvupd = nc.dram_tensor("wvup", [512, D], F32, kind="ExternalInput").ap()
    wod = nc.dram_tensor("wo", [D, D], F32, kind="ExternalInput").ap()
    wgated = nc.dram_tensor("wgate", [D, NE], F32, kind="ExternalInput").ap()
    bgated = nc.dram_tensor("bgate", [128, NE], F32, kind="ExternalInput").ap()
    W1d = nc.dram_tensor("W1", [NE, D, MLP], F32, kind="ExternalInput").ap()
    b1d = nc.dram_tensor("b1l", [128, NE * 32], F32, kind="ExternalInput").ap()
    W2d = nc.dram_tensor("W2", [NE, MLP, D], F32, kind="ExternalInput").ap()
    b2d = nc.dram_tensor("b2", [NE, D], F32, kind="ExternalInput").ap()
    identd = nc.dram_tensor("ident", [128, 128], F32, kind="ExternalInput").ap()
    onesd = nc.dram_tensor("onesm", [128, 128], F32, kind="ExternalInput").ap()
    proped = nc.dram_tensor("prope", [128, 128], F32, kind="ExternalInput").ap()
    triud = nc.dram_tensor("triu", [128, 128], F32, kind="ExternalInput").ap()
    iotagd = nc.dram_tensor("iotag", [128, 512], F32, kind="ExternalInput").ap()
    jcold = nc.dram_tensor("jcol", [128, 24], F32, kind="ExternalInput").ap()
    rcosd = nc.dram_tensor("rcos", [128, T], F32, kind="ExternalInput").ap()
    rsind = nc.dram_tensor("rsin", [128, T], F32, kind="ExternalInput").ap()
    yd = nc.dram_tensor("y", [T, D], F32, kind="ExternalOutput").ap()

    xd_t = xd.rearrange("(tc p) d -> p tc d", p=128)
    yd_t = yd.rearrange("(tc p) d -> p tc d", p=128)

    es = ExitStack()

    def sb(name, shape, dtype):
        return es.enter_context(nc.sbuf_tensor(name, shape, dtype)).ap()

    x_res = sb("x_res", [128, NTC, D], F32)
    ident_bf = sb("ident_bf", [128, 128], BF16)
    ident_f = sb("ident_f", [128, 128], F32)
    ones_bf = sb("ones_bf", [128, 128], BF16)
    prope_sb = sb("prope_sb", [128, 128], BF16)
    b1_sb = sb("b1_sb", [128, NE * 32], F32)
    b2_sb = sb("b2_sb", [NE, D], BF16)
    wgate_sb = sb("wgate_sb", [128, NE, NE], BF16)
    bgate_bc = sb("bgate_bc", [128, NE], F32)
    xn_tile = sb("xn_tile", [128, 2, D], BF16)
    stat = sb("stat", [128, 16], F32)
    lnst = sb("lnst", [128, NTC, 2], F32)
    logit_sb = sb("logit_sb", [128, NTC, NE], F32)
    gb1 = sb("gb1", [128, NTC, NE], F32)
    gb2 = sb("gb2", [128, NTC, NE], F32)
    gb3 = sb("gb3", [128, NTC, NE], F32)
    gexp_all = sb("gexp_all", [128, NTC, NE], F32)
    gmx = sb("gmx", [128, NTC], F32)
    gsum = sb("gsum", [128, NTC], F32)
    grp = sb("grp", [128, NTC], F32)
    coef_sb = sb("coef_sb", [128, NTC, NE], F32)
    triu_bf = sb("triu_bf", [128, 128], BF16)
    iotag_sb = sb("iotag_sb", [128, 384], F32)
    jcol_sb = sb("jcol_sb", [128, 24], F32)
    oh_bf = sb("oh_bf", [128, 128], BF16)
    dscr = sb("dscr", [128, 64], F32)
    dsc2 = sb("dsc2", [128, 128], F32)
    destT_sb = sb("destT_sb", [128, 128], F32)

    slab = sb("slab", [128, SLAB_HALF], BF16)

    def carve_bf(off_bytes, shape):
        n = int(np.prod(shape))
        v = slab[:, off_bytes // 2: off_bytes // 2 + n]
        if len(shape) == 1:
            return v
        if len(shape) == 2:
            return v.rearrange("p (a b) -> p a b", a=shape[0], b=shape[1])
        return v.rearrange("p (a b c) -> p a b c", a=shape[0], b=shape[1], c=shape[2])

    def carve_f32_flat(off_bytes, n):
        v = slab[:, off_bytes // 2: off_bytes // 2 + 2 * n].bitcast(F32)
        return v

    A_xntc = carve_bf(0, [2, 8, 128])
    A_xnTg = carve_bf(4096, [8, 512])
    A_qfg = carve_bf(12288, [8, 512])
    A_kfg = carve_bf(20480, [8, 512])
    A_aring = carve_bf(65536, [2, T])
    A_Mv = carve_bf(32768, [8, 1024])
    A_Mq = carve_bf(49152, [8, 512])
    A_Mk = carve_bf(57344, [8, 512])
    A_vf = carve_bf(98304, [16, D])
    F_xn_tm = carve_bf(0, [16, 1024])
    F_xnT = carve_bf(32768, [8, 128])
    F_w1 = carve_bf(34816, [2, 8, 1024])
    F_w2c = carve_bf(67584, [6, 1024])
    F_xgT = carve_bf(79872, [8, CAP])
    F_hgT = carve_bf(86016, [2, 8, CAP])
    F_yg = carve_bf(98304, [3, 1024])
    F_Gg = carve_bf(104448, [16, CAP])
    F_GsR = carve_bf(116736, [2, 3, 512])
    F_dbc = carve_f32_flat(122880, 2048).rearrange(
        "p (a b) -> p a b", a=16, b=128)
    F_coefT = carve_bf(86016, [T])
    B_cos = carve_bf(0, [1, T])
    B_sin = carve_bf(4096, [1, T])
    B_kr = carve_bf(8192, [8, T])
    B_v = carve_bf(40960, [16, 16, 65])
    B_latT = carve_bf(74240, [4, T])
    B_wd = carve_bf(90624, [8, 512])
    B_wku = carve_bf(98816, [4, 1024])
    B_wv = carve_bf(90624, [4, 1024])
    B_xn3T = carve_bf(107520, [8, 512])
    B_tmp = carve_bf(115712, [2, 512])
    B_wq = carve_bf(74240, [8, 1024])
    B_wo = carve_bf(90624, [8, 1024])
    B_qT = carve_bf(115712, [8, 512])
    B_At = carve_bf(123904, [2, 256])
    B_aoT = carve_bf(124928, [8, 256])
    B_rbx = carve_bf(129024, [64 * 4])
    rrow_sb = carve_bf(129536, [256])

    PS = es.enter_context(nc.psum_tensor("ps_all", [128, 4096], F32)).ap()

    def bank(i, n=1):
        return PS[:, i * 512:(i + n) * 512]

    def bank_bf(i):
        return PS.bitcast(BF16)[:, i * 1024:(i + 1) * 1024]

    sems = {k: es.enter_context(nc.semaphore("sem_" + k)) for k in ["T", "A", "V", "G", "S"]}

    sch = Sched()
    OP, WT = sch.op, sch.wt

    def dma_g(dst, src, wait=()):
        OP("G", lambda e, d=dst, s=src: e.dma_start(out=d, in_=s), wait=wait, inc="G", inc_k=16)

    def dma_s(dst, src, wait=()):
        OP("S", lambda e, d=dst, s=src: e.dma_start(out=d, in_=s), wait=wait, inc="S", inc_k=16)

    def mm_group(out, pairs, wait=()):
        n = len(pairs)
        for i, (l, r) in enumerate(pairs):
            OP("T", lambda e, o=out, l=l, r=r, st=(i == 0), sp=(i == n - 1):
               e.matmul(o, l, r, start=st, stop=sp),
               wait=(wait if i == 0 else ()), inc=("T" if i == n - 1 else None))

    def act(fn, wait=()):
        OP("A", fn, wait=wait, inc="A")

    def dve(fn, wait=()):
        OP("V", fn, wait=wait, inc="V")

    def barrier():
        vals = sch.barrier_vals()
        for eng in ["T", "A", "V", "G", "S"]:
            w = [(k, v) for k, v in vals if k != eng]
            OP(eng, lambda e: None, wait=w)

    def ring_guard(sem):
        return (sem, max(0, sch.cnt[sem] - 1))

    for i in range(4):
        dma_s(x_res[:, 4 * i:4 * i + 4, :], xd_t[:, 4 * i:4 * i + 4, :])
    x_loaded = WT("S")
    dve(lambda e: e.memset(stat[:, 14:15], EPS))
    dve(lambda e: e.memset(stat[:, 15:16], -C_STAB))
    dve(lambda e: e.memset(stat[:, 8:9], -C_STAB_A))
    dma_g(ident_bf, identd)
    dma_g(ident_f, identd)
    dma_g(ones_bf, onesd)
    dma_g(triu_bf, triud)
    dma_g(iotag_sb, iotagd[:, 0:384])
    dma_g(jcol_sb, jcold)
    dma_g(prope_sb, proped)
    dma_g(b1_sb, b1d)
    dma_g(b2_sb, b2d)
    dma_g(wgate_sb, wgated.rearrange("(c p) e -> p c e", p=128))
    dma_g(bgate_bc, bgated)
    dma_g(A_Mq, Mqd.rearrange("(c p) f -> p c f", p=128))
    dma_g(A_Mk, Mkd.rearrange("(c p) f -> p c f", p=128))
    dma_g(A_Mv, Mvd.rearrange("(c p) f -> p c f", p=128))
    w_loads0 = WT("G")

    def emit_ln(tc_list, dst_of, save_stats=False, use_saved=False, after_tile=None,
                apply_dst=None):
        for j, tc in enumerate(tc_list):
            ring = tc % 2
            ap_dst = apply_dst(tc) if apply_dst is not None else xn_tile[:, ring, :]
            if use_saved:
                rstd = lnst[:, tc, 0:1]
                nmr = lnst[:, tc, 1:2]
            else:
                rstd = stat[:, 0:1]
                nmr = stat[:, 1:2]
                s1 = stat[:, 2:3]
                sq = stat[:, 3:4]
                m = stat[:, 4:5]
                var = stat[:, 5:6]
                dve(lambda e, tc=tc, s1=s1: e.reduce_sum(s1, x_res[:, tc, :], axis=AX),
                    wait=([x_loaded] if j == 0 else ()))
                dve(lambda e, tc=tc, ring=ring, sq=sq: e.scalar_tensor_tensor(
                    xn_tile[:, ring, :], x_res[:, tc, :], 1.0, x_res[:, tc, :],
                    op0=ALU.bypass, op1=ALU.mult, accum_out=sq))
                dve(lambda e, s1=s1, m=m: e.tensor_scalar(m, s1, 1.0 / D, None, op0=ALU.mult))
                dve(lambda e, m=m, var=var: e.tensor_scalar(var, m, m[:, 0:1], None,
                                                            op0=ALU.mult))
                dve(lambda e, sq=sq, var=var: e.scalar_tensor_tensor(
                    var, sq, 1.0 / D, var, op0=ALU.mult, op1=ALU.subtract))
                act(lambda e, var=var: e.activation(stat[:, 6:7], var, AF.Ln,
                                                    bias=stat[:, 14:15], scale=1.0),
                    wait=[WT("V")])
                act(lambda e, rstd=rstd: e.activation(rstd, stat[:, 6:7], AF.Exp, bias=0.0, scale=-0.5))
                dve(lambda e, m=m, rstd=rstd, nmr=nmr: e.tensor_scalar(
                    nmr, m, rstd[:, 0:1], -1.0, op0=ALU.mult, op1=ALU.mult), wait=[WT("A")])
                if save_stats:
                    dve(lambda e, tc=tc, rstd=rstd: e.tensor_copy(lnst[:, tc, 0:1], rstd))
                    dve(lambda e, tc=tc, nmr=nmr: e.tensor_copy(lnst[:, tc, 1:2], nmr))
            act(lambda e, tc=tc, ap_dst=ap_dst, rstd=rstd, nmr=nmr: e.activation(
                ap_dst, x_res[:, tc, :], AF.Identity,
                bias=nmr[:, 0:1], scale=rstd[:, 0:1]),
                wait=[WT("V")])
            a_apply = WT("A")
            dst, co = dst_of(tc)
            for dmc in range(8):
                OP("T", lambda e, ap_dst=ap_dst, dmc=dmc: e.transpose(
                    bank_bf(7)[:, dmc * 128:(dmc + 1) * 128],
                    ap_dst[:, dmc * 128:(dmc + 1) * 128], ident_bf),
                   wait=([a_apply, WT("V")] if dmc == 0 else ()),
                   inc=("T" if dmc == 7 else None))
            tdone = WT("T")
            dve(lambda e, co=co, dst=dst: e.tensor_copy(
                dst[:, :, co:co + 128],
                bank_bf(7).rearrange("p (c f) -> p c f", c=8)),
                wait=[tdone])
            if after_tile is not None:
                after_tile(tc)

    def emit_vf(tc):
        ring = tc % 2
        cp_V = WT("V")
        for half in range(2):
            rb = (tc * 2 + half) % 2
            w = [cp_V, w_loads0] if half == 0 else []
            w.append(ring_guard("A"))
            mm_group(bank(rb),
                     [(A_xntc[:, ring, dmc, :], A_Mv[:, dmc, half * 512:(half + 1) * 512])
                      for dmc in range(8)], wait=w)
            mmil = WT("T")
            act(lambda e, tc=tc, half=half, rb=rb: e.activation(
                A_vf[:, tc, half * 512:(half + 1) * 512], bank(rb), AF.Copy,
                bias=0.0, scale=1.0), wait=[mmil])

    emit_ln(list(range(NTC)), lambda tc: (A_xntc[:, tc % 2], 0),
            save_stats=True, after_tile=emit_vf)
    p1_done_A = WT("A")
    p1_done_T = WT("T")

    if n_sub >= 1:
        mile_sum = 0
        for gp in range(4):
            emit_ln([4 * gp + j for j in range(4)],
                    lambda tc: (A_xnTg, (tc % 4) * 128), use_saved=True)
            lng_V = WT("V")
            for which, Msb, dstg, scale in ((0, A_Mq, A_qfg, 0.125),
                                            (1, A_Mk, A_kfg, 1.0)):
                for s in range(8):
                    rb = s % 2
                    w = [lng_V] if (which == 0 and s == 0) else []
                    w.append(ring_guard("A"))
                    mm_group(bank(rb)[0:64, 0:512],
                             [(Msb[:, dmc, 64 * s:64 * s + 64], A_xnTg[:, dmc, :])
                              for dmc in range(8)], wait=w)
                    mmil = WT("T")
                    act(lambda e, s=s, rb=rb, dstg=dstg, scale=scale: e.activation(
                        dstg[0:64, s, :], bank(rb)[0:64, 0:512], AF.Copy,
                        bias=0.0, scale=scale), wait=[mmil])
            qk_A = WT("A")
            for gl in range(2):
              g = 2 * gp + gl
              if True:
                mile_exp_h = {}
                mile_norm = {}

              def emit_av(j):
                  jr = j % 2
                  for rc in range(2):
                      for h2 in range(8):
                          bi = 4 + rc * 2 + h2 // 4
                          off = (h2 % 4) * 128
                          st = (j == 0) and (h2 % 4 == 0)
                          sp = (j == 15) and (h2 % 4 == 3)
                          w = [("V", mile_norm[j])] if (rc == 0 and h2 == 0) else ()
                          OP("T", lambda e, j=j, rc=rc, h2=h2, bi=bi, off=off, st=st,
                             sp=sp, jr=jr: e.matmul(
                              bank(bi)[:, off:off + 128],
                              A_aring[:, jr, 256 * h2 + 128 * rc: 256 * h2 + 128 * rc + 128],
                              A_vf[:, 2 * h2 + (j % 2), 128 * (j // 2):128 * (j // 2) + 128],
                              start=st, stop=sp),
                             wait=w, inc=("T" if (rc == 1 and h2 == 7) else None))

              for m in range(16):
                  ring = m % 2
                  tS = m // 2
                  for s in range(8):
                      bi = s // 2
                      off = (s % 2) * 256
                      w = []
                      if s == 0 or s == 4:
                          half = s // 4
                          if m == 0:
                              w.append(("A", sch.cnt["A"]))
                          else:
                              w.append(("A", mile_exp_h[(m - 1, half)]))
                      OP("T", lambda e, m=m, s=s, bi=bi, off=off, tS=tS, gl=gl: e.matmul(
                          bank(bi)[:, off:off + 256],
                          A_kfg[0:64, tS, 256 * gl + 128 * (m % 2):256 * gl + 128 * (m % 2) + 128],
                          A_qfg[0:64, s, 256 * gl:256 * gl + 256],
                          start=True, stop=True),
                         wait=w, inc=("T" if (s == 3 or s == 7) else None))
                      if s == 3 or s == 7:
                          half = s // 4
                          schalf_T = WT("T")
                          wexp = [schalf_T]
                          if half == 0 and m >= 1:
                              wexp.append(("V", mile_sum))
                          act(lambda e, ring=ring, half=half: e.activation(
                              A_aring[:, ring, half * 1024:(half + 1) * 1024],
                              PS[:, half * 1024:(half + 1) * 1024], AF.Exp,
                              bias=stat[:, 8:9], scale=1.0,
                              accum_out=stat[:, 9 + half:10 + half]),
                              wait=wexp)
                          mile_exp_h[(m, half)] = sch.cnt["A"]
                  dve(lambda e: e.tensor_scalar(stat[:, 11:12], stat[:, 9:10],
                                                stat[:, 10:11], None, op0=ALU.add),
                      wait=[("A", mile_exp_h[(m, 1)])])
                  mile_sum = sch.cnt["V"]
                  dve(lambda e: e.reciprocal(stat[:, 12:13], stat[:, 11:12]))
                  dve(lambda e, ring=ring: e.tensor_scalar(
                      A_aring[:, ring, :], A_aring[:, ring, :], stat[:, 12:13], None,
                      op0=ALU.mult))
                  mile_norm[m] = sch.cnt["V"]
                  if m >= 1:
                      emit_av(m - 1)
              emit_av(15)
              av_T = WT("T")
              for rc in range(2):
                  dve(lambda e, g=g, rc=rc: e.tensor_tensor(
                      x_res[:, 2 * g + rc, :], bank(4 + rc * 2, 2),
                      x_res[:, 2 * g + rc, :], op=ALU.add),
                      wait=([av_T] if rc == 0 else ()))
              evg_V = WT("V")
              OP("T", lambda e: None, wait=[evg_V])
        barrier()

    def emit_ffn():
        def gate_after(tc):
            gbk = 6 + tc % 2
            mm_group(bank(gbk)[:, 0:8],
                     [(F_xnT[:, dmc, :], wgate_sb[:, dmc, :]) for dmc in range(8)],
                     wait=[WT("V")])
            gm_T = WT("T")
            dve(lambda e, tc=tc, gbk=gbk: e.tensor_tensor(
                logit_sb[:, tc, :], bank(gbk)[:, 0:8], bgate_bc, op=ALU.add),
                wait=[gm_T])

        emit_ln(list(range(NTC)), lambda tc: (F_xnT, 0),
                apply_dst=lambda tc: F_xn_tm[:, tc, :], after_tile=gate_after)
        def bc16(v):
            return v.rearrange("p (a o) -> p a o", o=1).broadcast_to((128, NTC, NE))
        dve(lambda e: e.reduce_max(gmx, logit_sb, axis=AX, negate=True))
        dve(lambda e: e.tensor_tensor(gb1, logit_sb, bc16(gmx), op=ALU.add))
        gb1_V = WT("V")
        act(lambda e: e.activation(
            gexp_all.rearrange("p a b -> p (a b)"), gb1.rearrange("p a b -> p (a b)"),
            AF.Exp, bias=0.0, scale=1.0), wait=[gb1_V])
        exps_A = WT("A")
        dve(lambda e: e.tensor_reduce(gsum, gexp_all, axis=AX, op=ALU.add),
            wait=[exps_A])
        dve(lambda e: e.reciprocal(grp, gsum))
        dve(lambda e: e.tensor_scalar(gb2, gb1, 0.0, None, op0=ALU.is_equal))
        dve(lambda e: e.memset(gb3.rearrange("p a b -> p (a b)"), 0.0))
        dve(lambda e: e.tensor_copy(gb3[:, :, 1:8], gb2[:, :, 0:7]))
        dve(lambda e: e.tensor_copy(gb1, gb3))
        dve(lambda e: e.tensor_tensor(gb1[:, :, 2:8], gb3[:, :, 2:8], gb3[:, :, 0:6],
                                      op=ALU.add))
        dve(lambda e: e.tensor_copy(gb3, gb1))
        dve(lambda e: e.tensor_tensor(gb3[:, :, 4:8], gb1[:, :, 4:8], gb1[:, :, 0:4],
                                      op=ALU.add))
        dve(lambda e: e.tensor_scalar(gb1, gb3, 0.0, None, op0=ALU.is_equal))
        dve(lambda e: e.tensor_tensor(gb2, gb2, gb1, op=ALU.mult))
        dve(lambda e: e.tensor_tensor(coef_sb, gb2, bc16(grp), op=ALU.mult))
        cf_V = WT("V")

        gb2v = gb2.rearrange("p a b -> p (a b)")
        gb1v = gb1.rearrange("p a b -> p (a b)")
        gb3v = gb3.rearrange("p a b -> p (a b)")
        dve(lambda e: e.tensor_copy(oh_bf, gb2v))
        dve(lambda e: e.tensor_copy(dsc2[:, 0:8], iotag_sb[:, 0:8]))
        for w_ in (8, 16, 32, 64):
            dve(lambda e, w_=w_: e.tensor_copy(dsc2[:, w_:2 * w_], dsc2[:, 0:w_]))
        dve(lambda e: e.tensor_tensor(dsc2, dsc2, gb2v, op=ALU.mult))
        dve(lambda e: e.tensor_reduce(
            dscr[:, 0:16], dsc2.rearrange("p (a b) -> p a b", a=NTC),
            axis=AX, op=ALU.add))
        ohb_V = WT("V")
        OP("T", lambda e: e.matmul(bank(0)[:, 0:128], triu_bf, oh_bf,
                                   start=True, stop=True), wait=[ohb_V], inc="T")
        OP("T", lambda e: e.matmul(bank(1)[0:1, 0:128], ones_bf[:, 0:1], oh_bf,
                                   start=True, stop=True), inc="T")
        pref_T = WT("T")
        dve(lambda e: e.memset(gb1v[0:1, 0:8], 0.0), wait=[pref_T])
        dve(lambda e: e.tensor_copy(gb1v[0:1, 8:128], bank(1)[0:1, 0:120]))
        bufs = [gb1v, gb3v]
        cur = 0
        for k in (8, 16, 32, 64):
            a, b = bufs[cur], bufs[1 - cur]
            dve(lambda e, a=a, b=b, k=k: e.tensor_copy(
                b[0:1, 0:k], a[0:1, 0:k]))
            dve(lambda e, a=a, b=b, k=k: e.tensor_tensor(
                b[0:1, k:128], a[0:1, k:128],
                a[0:1, 0:128 - k], op=ALU.add))
            cur = 1 - cur
        scan_buf = bufs[cur]
        dve(lambda e: e.memset(dsc2[0:1, 0:128], 1.0))
        scan_V = WT("V")
        OP("T", lambda e, a=scan_buf: e.matmul(
            bank(1)[:, 0:128], dsc2[0:1, 0:128],
            a[0:1, 0:128], start=True, stop=True),
           wait=[scan_V], inc="T")
        base_T = WT("T")
        dve(lambda e: e.tensor_tensor(dsc2, bank(0)[:, 0:128], gb2v, op=ALU.mult),
            wait=[base_T])
        dve(lambda e: e.tensor_reduce(
            dscr[:, 16:32], dsc2.rearrange("p (a b) -> p a b", a=NTC),
            axis=AX, op=ALU.add))
        dve(lambda e: e.tensor_tensor(dsc2, bank(1)[:, 0:128], gb2v, op=ALU.mult))
        dve(lambda e: e.tensor_reduce(
            dscr[:, 48:64], dsc2.rearrange("p (a b) -> p a b", a=NTC),
            axis=AX, op=ALU.add))
        dve(lambda e: e.tensor_tensor(dscr[:, 16:32], dscr[:, 16:32], dscr[:, 48:64],
                                      op=ALU.add))
        dve(lambda e: e.tensor_scalar(dscr[:, 16:32], dscr[:, 16:32], float(CAP - 1),
                                      None, op0=ALU.min))
        dve(lambda e: e.scalar_tensor_tensor(dscr[:, 32:48], dscr[:, 0:16], float(CAP),
                                             dscr[:, 16:32], op0=ALU.mult, op1=ALU.add))
        dve(lambda e: e.memset(dsc2[0:1, 0:128], 1.0))
        dest_V = WT("V")
        OP("T", lambda e: e.transpose(bank(6)[0:16, 0:128], dscr[:, 32:48], ident_f),
           wait=[dest_V], inc="T")
        dtr_T = WT("T")
        dve(lambda e: e.tensor_copy(destT_sb[0:16, 0:128], bank(6)[0:16, 0:128]),
            wait=[dtr_T])
        dtc_V = WT("V")
        dma_s(F_dbc[0:1].rearrange("o a b -> o (a b)"), destT_sb[0:16, 0:128],
              wait=[dtc_V])
        drow_S = WT("S")
        for q in range(4):
            OP("T", lambda e, q=q: e.matmul(
                bank(2 + q)[:, 0:512], dsc2[0:1, 0:128],
                F_dbc[0:1].rearrange("o a b -> o (a b)")[:, q * 512:(q + 1) * 512],
                start=True, stop=True),
               wait=([drow_S] if q == 0 else ()), inc="T")
            qT = WT("T")
            dve(lambda e, q=q: e.tensor_copy(
                F_dbc[:, 4 * q:4 * q + 4, :].rearrange("p a b -> p (a b)"),
                bank(2 + q)[:, 0:512]), wait=[qT])
        dbc_V = WT("V")

        for tc in range(NTC):
            OP("T", lambda e, tc=tc: e.transpose(bank(7)[0:8, 0:128], coef_sb[:, tc, :],
                                                 ident_f),
               wait=([cf_V] if tc == 0 else [("V", sch.cnt["V"])]), inc="T")
            dve(lambda e, tc=tc: e.tensor_copy(
                F_coefT[0:8, tc * 128:(tc + 1) * 128], bank(7)[0:8, 0:128]),
                wait=[WT("T")])
        for tc in range(NTC):
            w = [("V", sch.cnt["V"])]
            for hh in range(2):
                OP("T", lambda e, tc=tc, hh=hh: e.matmul(
                    bank(4 + hh), F_coefT[0:8, tc * 128:(tc + 1) * 128],
                    b2_sb[:, hh * 512:(hh + 1) * 512], start=True, stop=True),
                   wait=(w if hh == 0 else ()), inc="T")
            b2_T = WT("T")
            dve(lambda e, tc=tc: e.tensor_tensor(
                x_res[:, tc, :], bank(4, 2), x_res[:, tc, :], op=ALU.add), wait=[b2_T])
        b2_done_T = sch.cnt["T"]
        b2_done_V = WT("V")

        w1_free = [0, 0]
        hgT_free = [b2_done_T, b2_done_T]
        w2_lastread = [0] * 6
        w2_loaded = {}
        ge_T = 0
        xg_A = 0
        yg_A = 0
        w2_order = [(ee, mm, cc) for ee in range(NE) for mm in range(4)
                    for cc in range(8)]
        w2_next = [0]

        def pump_w2(upto):
            while w2_next[0] < min(len(w2_order), upto):
                gci = w2_next[0]
                e2, mg2, mc82 = w2_order[gci]
                ci = gci % 6
                dma_g(F_w2c[:, ci, :],
                      W2d[e2][mg2 * 1024 + mc82 * 128: mg2 * 1024 + (mc82 + 1) * 128, :],
                      wait=[("T", w2_lastread[ci])])
                w2_loaded[gci] = WT("G")
                w2_next[0] += 1

        pump_w2(6)
        for e_i in range(NE):
            dve(lambda e, e_i=e_i: e.tensor_scalar(
                dscr[:, 48:64], dscr[:, 32:48], float(-CAP * e_i), None, op0=ALU.add),
                wait=[("T", ge_T), dest_V] if e_i > 0 else [dbc_V, dest_V])
            for tc in range(NTC):
                dve(lambda e, tc=tc: e.tensor_scalar(
                    F_Gg[:, tc, :], iotag_sb[:, 0:CAP], dscr[:, 48 + tc:49 + tc],
                    None, op0=ALU.is_equal))
            gg_V = WT("V")
            for dmc in range(8):
                for tc in range(NTC):
                    w = []
                    if tc == 0:
                        w.append(gg_V)
                        if dmc == 0:
                            w.append(("A", sch.cnt["A"]))
                        elif dmc >= 2:
                            w.append(("A", xg_A))
                    OP("T", lambda e, dmc=dmc, tc=tc: e.matmul(
                        bank(dmc % 2)[:, 0:CAP],
                        F_xn_tm[:, tc, dmc * 128:(dmc + 1) * 128], F_Gg[:, tc, :],
                        start=(tc == 0), stop=(tc == NTC - 1)),
                       wait=w, inc=("T" if tc == NTC - 1 else None))
                g_T = WT("T")
                act(lambda e, dmc=dmc: e.activation(
                    F_xgT[:, dmc, :], bank(dmc % 2)[:, 0:CAP], AF.Copy,
                    bias=0.0, scale=1.0), wait=[g_T])
                xg_A = sch.cnt["A"]
            ge_T = sch.cnt["T"]

            hg_A = {}
            y_done = {}

            def emit_h(mg, e_i=e_i):
                slot = mg % 2
                for dmc in range(8):
                    dma_g(F_w1[:, slot, dmc, :],
                          W1d[e_i][dmc * 128:(dmc + 1) * 128, mg * 1024:(mg + 1) * 1024],
                          wait=([("T", w1_free[slot])] if dmc == 0 else ()))
                w1_S = WT("G")
                for mc8 in range(8):
                    w = []
                    if mc8 == 0:
                        w = [w1_S, ("A", xg_A)]
                    elif mc8 >= 2:
                        w = [("A", hg_A[(mg, mc8 - 2)])]
                    for dmc in range(8):
                        OP("T", lambda e, mc8=mc8, dmc=dmc, slot=slot: e.matmul(
                            bank(mc8 % 2)[:, 0:CAP],
                            F_w1[:, slot, dmc, mc8 * 128:(mc8 + 1) * 128],
                            F_xgT[:, dmc, :],
                            start=(dmc == 0), stop=(dmc == 7)),
                           wait=(w if dmc == 0 else ()),
                           inc=("T" if dmc == 7 else None))
                    hm_T = WT("T")
                    wv = [hm_T]
                    if mc8 == 0:
                        wv.append(("T", hgT_free[slot]))
                    act(lambda e, e_i=e_i, mg=mg, mc8=mc8, slot=slot: e.activation(
                        F_hgT[:, slot, mc8, :], bank(mc8 % 2)[:, 0:CAP], AF.Relu,
                        bias=b1_sb[:, e_i * 32 + mg * 8 + mc8: e_i * 32 + mg * 8 + mc8 + 1],
                        scale=1.0), wait=wv)
                    hg_A[(mg, mc8)] = sch.cnt["A"]
                w1_free[slot] = sch.cnt["T"]

            def emit_y(mg, e_i=e_i):
                slot = mg % 2
                for mc8 in range(8):
                    gci = (e_i * 4 + mg) * 8 + mc8
                    ci = gci % 6
                    for jt in range(3):
                        for hh in range(2):
                            w = []
                            if jt == 0 and hh == 0:
                                w.append(w2_loaded[gci])
                                if mc8 == 0:
                                    w.append(("A", hg_A[(mg, 7)]))
                                    if mg == 0:
                                        w.append(("A", sch.cnt["A"]))
                                        w.append(("V", sch.cnt["V"]))
                            OP("T", lambda e, mg=mg, mc8=mc8, jt=jt, hh=hh, ci=ci,
                               slot=slot: e.matmul(
                                bank(2 + 2 * jt + hh),
                                F_hgT[:, slot, mc8, jt * 128:(jt + 1) * 128],
                                F_w2c[:, ci, hh * 512:(hh + 1) * 512],
                                start=(mg == 0 and mc8 == 0),
                                stop=(mg == 3 and mc8 == 7)),
                               wait=w, inc=("T" if (jt == 2 and hh == 1) else None))
                    w2_lastread[ci] = sch.cnt["T"]
                    pump_w2(gci + 7)
                y_done[mg] = sch.cnt["T"]
                hgT_free[slot] = sch.cnt["T"]

            emit_h(0)
            emit_h(1)
            emit_y(0)
            emit_h(2)
            emit_y(1)
            emit_h(3)
            emit_y(2)
            emit_y(3)
            yl_T = WT("T")
            for jt in range(3):
                for hh in range(2):
                    act(lambda e, jt=jt, hh=hh: e.activation(
                        F_yg[:, jt, hh * 512:(hh + 1) * 512],
                        bank(2 + 2 * jt + hh), AF.Copy, bias=0.0, scale=1.0),
                        wait=([yl_T] if (jt == 0 and hh == 0) else ()))
            yg_A = sch.cnt["A"]

            gs_V = {}
            ps_A = {}
            tt_V = {}
            sc_T = {}
            for tc in range(NTC):
                q = tc // 4
                if tc % 4 == 0:
                    for jc in range(3):
                        wv = [("T", sc_T[q - 2])] if q >= 2 else []
                        dve(lambda e, q=q, jc=jc, e_i=e_i: e.tensor_scalar(
                            F_GsR[:, q % 2, jc, :],
                            F_dbc[:, 4 * q:4 * q + 4, :].rearrange("p a b -> p (a b)"),
                            jcol_sb[:, 3 * e_i + jc:3 * e_i + jc + 1],
                            None, op0=ALU.is_equal), wait=wv)
                    gs_V[q] = WT("V")
                par = tc % 2
                for hh in range(2):
                    w = []
                    if hh == 0:
                        w.append(gs_V[q])
                        if tc == 0:
                            w.append(("A", yg_A))
                        if tc >= 2:
                            w.append(("A", ps_A[(tc - 2, 1)]))
                    for jc in range(3):
                        OP("T", lambda e, tc=tc, hh=hh, jc=jc, q=q, par=par: e.matmul(
                            bank(2 * par + hh),
                            F_GsR[:, q % 2, jc, (tc % 4) * 128:(tc % 4 + 1) * 128],
                            F_yg[:, jc, hh * 512:(hh + 1) * 512],
                            start=(jc == 0), stop=(jc == 2)),
                           wait=(w if jc == 0 else ()),
                           inc=("T" if jc == 2 else None))
                    s_T = WT("T")
                    wp = [s_T]
                    if tc >= 2:
                        wp.append(("V", tt_V[tc - 2]))
                    act(lambda e, tc=tc, hh=hh, par=par, e_i=e_i: e.activation(
                        bank(4 + 2 * par + hh), bank(2 * par + hh), AF.Copy,
                        bias=0.0, scale=coef_sb[:, tc, e_i:e_i + 1]), wait=wp)
                    ps_A[(tc, hh)] = sch.cnt["A"]
                sc_T[tc // 4] = sch.cnt["T"]
                dve(lambda e, tc=tc, par=par: e.tensor_tensor(
                    x_res[:, tc, :], bank(4 + 2 * par, 2), x_res[:, tc, :],
                    op=ALU.add), wait=[("A", ps_A[(tc, 1)])])
                tt_V[tc] = sch.cnt["V"]

    if n_sub >= 2:
        emit_ffn()
        barrier()

    if n_sub >= 3:
        dma_g(B_cos[:, 0, :], rcosd)
        dma_g(B_sin[:, 0, :], rsind)
        dma_g(B_wd, wdownd.rearrange("(c p) f -> p c f", p=128))
        dma_g(B_wku, wkupd.rearrange("(c p) f -> p c f", p=128))
        b1_loads = WT("G")
        for tl in range(4):
            emit_ln([4 * tl + j for j in range(4)],
                    lambda tc: (B_xn3T, (tc % 4) * 128), save_stats=True)
            ln3_V = WT("V")
            for lfc in range(4):
                rb = lfc % 2
                w = [b1_loads, ln3_V] if lfc == 0 else []
                w.append(ring_guard("A"))
                mm_group(bank(rb),
                         [(B_wd[:, dmc, lfc * 128:(lfc + 1) * 128],
                           B_xn3T[:, dmc, :]) for dmc in range(8)], wait=w)
                lm_T = WT("T")
                act(lambda e, tl=tl, lfc=lfc, rb=rb: e.activation(
                    B_latT[:, lfc, tl * 512:(tl + 1) * 512], bank(rb), AF.Copy,
                    bias=0.0, scale=1.0), wait=[lm_T])
        latT_A = WT("A")
        for tt in range(4):
            for kfc in range(8):
                rb = (tt * 8 + kfc) % 2
                w = [latT_A] if (tt == 0 and kfc == 0) else []
                w.append(ring_guard("A"))
                mm_group(bank(rb),
                         [(B_wku[:, lfc, kfc * 128:(kfc + 1) * 128],
                           B_latT[:, lfc, tt * 512:(tt + 1) * 512]) for lfc in range(4)],
                         wait=w)
                km_T = WT("T")
                act(lambda e, tt=tt, kfc=kfc, rb=rb: e.activation(
                    B_kr[:, kfc, tt * 512:(tt + 1) * 512], bank(rb), AF.Copy,
                    bias=0.0, scale=1.0), wait=[km_T])
        kT_done_A = WT("A")
        dma_g(B_wv, wvupd.rearrange("(c p) f -> p c f", p=128), wait=[WT("T")])
        wv_G = WT("G")
        for tc in range(NTC):
            for hh in range(2):
                rb = (tc * 2 + hh) % 2
                w = [wv_G, kT_done_A] if (tc == 0 and hh == 0) else []
                w.append(ring_guard("A"))
                mm_group(bank(rb),
                         [(B_latT[:, lfc, tc * 128:(tc + 1) * 128],
                           B_wv[:, lfc, hh * 512:(hh + 1) * 512]) for lfc in range(4)],
                         wait=w)
                vm_T = WT("T")
                act(lambda e, tc=tc, hh=hh, rb=rb: e.activation(
                    B_v[:, tc, 8 * hh:8 * hh + 8, 0:64],
                    bank(rb), AF.Copy, bias=0.0, scale=1.0), wait=[vm_T])
        dve(lambda e: e.memset(B_v[:, :, :, 64:65], 1.0))
        v_done_T = WT("T")
        v_done_A = WT("A")
        v_ones_V = WT("V")
        for kfc in range(8):
            for tt in range(4):
                rb = 2 + (kfc * 4 + tt) % 2
                w = [kT_done_A, b1_loads] if (kfc == 0 and tt == 0) else []
                w.append(("V", max(0, sch.cnt["V"] - 2)))
                OP("T", lambda e, kfc=kfc, tt=tt, rb=rb: e.matmul(
                    bank(rb), prope_sb, B_kr[:, kfc, tt * 512:(tt + 1) * 512],
                    start=True, stop=True), wait=w, inc="T")
                sw_T = WT("T")
                dve(lambda e, kfc=kfc, tt=tt: e.tensor_tensor(
                    B_tmp[:, 0, :], B_kr[:, kfc, tt * 512:(tt + 1) * 512],
                    B_cos[:, 0, tt * 512:(tt + 1) * 512], op=ALU.mult), wait=[sw_T])
                dve(lambda e, tt=tt, rb=rb: e.tensor_tensor(
                    B_tmp[:, 1, :], bank(rb),
                    B_sin[:, 0, tt * 512:(tt + 1) * 512], op=ALU.mult))
                dve(lambda e, kfc=kfc, tt=tt: e.tensor_tensor(
                    B_kr[:, kfc, tt * 512:(tt + 1) * 512], B_tmp[:, 0, :],
                    B_tmp[:, 1, :], op=ALU.add))
        krope_V = WT("V")
        dma_g(B_wq, wqd.rearrange("(c p) f -> p c f", p=128), wait=[v_done_T])
        dma_g(B_wo, wod.rearrange("(c p) f -> p c f", p=128), wait=[krope_V])
        b2w_G = WT("G")

        for tl in range(4):
            emit_ln([4 * tl + j for j in range(4)],
                    lambda tc: (B_xn3T, (tc % 4) * 128), use_saved=True)
            ln3b_V = WT("V")
            for qfc in range(8):
                rb = qfc % 2
                w = [b2w_G, ln3b_V] if qfc == 0 else []
                w.append(ring_guard("A"))
                mm_group(bank(rb),
                         [(B_wq[:, dmc, qfc * 128:(qfc + 1) * 128],
                           B_xn3T[:, dmc, :]) for dmc in range(8)], wait=w)
                qm_T = WT("T")
                act(lambda e, qfc=qfc, rb=rb: e.activation(
                    B_qT[:, qfc, :], bank(rb), AF.Copy, bias=0.0, scale=0.125),
                    wait=[qm_T])
            qT_A = WT("A")
            for qfc in range(8):
                rb = qfc % 2
                w = [qT_A] if qfc == 0 else []
                w.append(("V", max(0, sch.cnt["V"] - 2)))
                OP("T", lambda e, qfc=qfc, rb=rb: e.matmul(
                    bank(rb), prope_sb, B_qT[:, qfc, :], start=True, stop=True),
                   wait=w, inc="T")
                swq_T = WT("T")
                dve(lambda e, qfc=qfc, tl=tl: e.tensor_tensor(
                    B_xn3T[:, 0, :], B_qT[:, qfc, :],
                    B_cos[:, 0, tl * 512:(tl + 1) * 512], op=ALU.mult), wait=[swq_T])
                dve(lambda e, rb=rb, tl=tl: e.tensor_tensor(
                    B_xn3T[:, 1, :], bank(rb),
                    B_sin[:, 0, tl * 512:(tl + 1) * 512], op=ALU.mult))
                dve(lambda e, qfc=qfc: e.tensor_tensor(
                    B_qT[:, qfc, :], B_xn3T[:, 0, :], B_xn3T[:, 1, :], op=ALU.add))
            qrope_V = WT("V")

            for pg_l in range(2):
                pg = tl * 2 + pg_l
                kmax_c = 2 * (pg + 1)
                for h in range(NH_B):
                    mile_eb = {}

                    def emit_avb(kc, h=h, kmax_c=kmax_c):
                        w_av = [("A", mile_eb[kc])]
                        if kc == 0:
                            w_av.append(("V", sch.cnt["V"]))
                        OP("T", lambda e, h=h, kc=kc, st=(kc == 0),
                           sp=(kc == kmax_c - 1):
                           e.matmul(bank(4)[0:65, 0:256],
                                    B_v[:, kc, h, :], B_At[:, kc % 2, :],
                                    start=st, stop=sp), wait=w_av, inc="T")

                    for kc in range(kmax_c):
                        rS = 2 + kc % 2
                        w = []
                        if h == 0 and kc == 0:
                            w = [qrope_V, v_done_A]
                        w.append(("A", max(0, sch.cnt["A"] - 1)))
                        OP("T", lambda e, h=h, kc=kc, rS=rS, pg_l=pg_l: e.matmul(
                            bank(rS)[:, 0:256],
                            B_kr[64 * (h % 2):64 * (h % 2) + 64, h // 2,
                                 kc * 128:(kc + 1) * 128],
                            B_qT[64 * (h % 2):64 * (h % 2) + 64, h // 2,
                                 pg_l * 256:(pg_l + 1) * 256],
                            start=True, stop=True), wait=w, inc="T")
                        s_T = WT("T")
                        act(lambda e, kc=kc, rS=rS: e.activation(
                            B_At[:, kc % 2, :], bank(rS)[:, 0:256], AF.Exp,
                            bias=stat[:, 15:16], scale=1.0), wait=[s_T])
                        mile_eb[kc] = sch.cnt["A"]
                        if kc >= 1:
                            emit_avb(kc - 1)
                    emit_avb(kmax_c - 1)
                    av_T = WT("T")

                    def _recip_lp(e):
                        with nc.allow_low_precision(reason="bf16 softmax normaliser"):
                            return e.reciprocal(rrow_sb[0:1, 0:256],
                                                bank(4)[64:65, 0:256])
                    dve(_recip_lp, wait=[av_T])
                    rr_V = WT("V")
                    OP("T", lambda e: e.matmul(bank(6)[0:64, 0:256], ones_bf[0:1, 0:64],
                                               rrow_sb[0:1, 0:256], start=True, stop=True),
                       wait=[rr_V], inc="T")
                    rb_T = WT("T")
                    dve(lambda e: e.tensor_copy(B_rbx[0:64, 0:256],
                                                bank(6)[0:64, 0:256]),
                        wait=[rb_T])
                    dve(lambda e, h=h: e.tensor_tensor(
                        B_aoT[64 * (h % 2):64 * (h % 2) + 64, h // 2, :],
                        bank(4)[0:64, 0:256], B_rbx[0:64, 0:256], op=ALU.mult))
                ao_V = WT("V")
                for tc2 in range(2):
                    tc = pg * 2 + tc2
                    for hh in range(2):
                        w = [ao_V, b2w_G] if (tc2 == 0 and hh == 0) else []
                        if tc2 == 1 and hh == 0:
                            w.append(("V", sch.cnt["V"]))
                        mm_group(bank(hh),
                                 [(B_aoT[:, e1c, tc2 * 128:(tc2 + 1) * 128],
                                   B_wo[:, e1c, hh * 512:(hh + 1) * 512])
                                  for e1c in range(8)], wait=w)
                    wo_T = WT("T")
                    dve(lambda e, tc=tc: e.tensor_tensor(
                        x_res[:, tc, :], bank(0, 2), x_res[:, tc, :], op=ALU.add),
                        wait=[wo_T])
                pg_V = WT("V")
                OP("T", lambda e: None, wait=[pg_V])
        barrier()

    if n_sub >= 4:
        emit_ffn()
        barrier()

    final_w = sch.barrier_vals()
    for i in range(4):
        dma_s(yd_t[:, 4 * i:4 * i + 4, :], x_res[:, 4 * i:4 * i + 4, :],
              wait=(final_w if i == 0 else ()))
    if dbg:
        dmp_slab = nc.dram_tensor("dmp_slab", [128, SLAB_HALF], BF16,
                                  kind="ExternalOutput").ap()
        dmp_stat = nc.dram_tensor("dmp_stat", [128, 16 + NTC * 2], F32,
                                  kind="ExternalOutput").ap()
        for i in range(8):
            n8 = SLAB_HALF // 8
            dma_s(dmp_slab[:, i * n8:(i + 1) * n8], slab[:, i * n8:(i + 1) * n8],
                  wait=(final_w if i == 0 else ()))
        dma_s(dmp_stat[:, 0:16], stat)
        dma_s(dmp_stat[:, 16:16 + NTC * 2],
              lnst.rearrange("p a b -> p (a b)"))
    OP("S", lambda e: None, wait=[WT("S")])

    block = es.enter_context(nc.Block())

    def emit_engine(code, eng):
        last_wait = {}
        own = 0
        for engine, fn, waits, inc, inc_k in sch.ops:
            if engine != code:
                continue
            allw = list(waits)
            if code in ("V", "A") and own > 0:
                allw.append((code, own))
            for (sname, val) in allw:
                if val <= 0:
                    continue
                if last_wait.get(sname, -1) >= val:
                    continue
                eng.wait_ge(sems[sname], val)
                last_wait[sname] = val
            instr = fn(eng)
            if inc is not None:
                assert instr is not None
                instr.then_inc(sems[inc], inc_k)
                if code in ("V", "A"):
                    own += inc_k

    @block.sync
    def _(e):
        emit_engine("S", e)

    @block.gpsimd
    def _(e):
        emit_engine("G", e)

    @block.tensor
    def _(e):
        emit_engine("T", e)

    @block.scalar
    def _(e):
        emit_engine("A", e)

    @block.vector
    def _(e):
        emit_engine("V", e)

    es.close()
    return nc


def _consts():
    ident = np.eye(128, dtype=np.float32)
    onesm = np.ones((128, 128), dtype=np.float32)
    prope = np.zeros((128, 128), dtype=np.float32)
    for i in range(64):
        prope[2 * i + 1, 2 * i] = -1.0
        prope[2 * i, 2 * i + 1] = 1.0
    inv = 1.0 / (10000.0 ** (np.arange(0, 64, 2, dtype=np.float32) / 64.0))
    ang = np.outer(np.arange(T, dtype=np.float32), inv)
    cos = np.cos(ang).astype(np.float32)
    sin = np.sin(ang).astype(np.float32)
    fi = (np.arange(128) % 64) // 2
    rcos = np.ascontiguousarray(cos[:, fi].T)
    rsin = np.ascontiguousarray(sin[:, fi].T)
    triu = np.triu(np.ones((128, 128), dtype=np.float32), 1)
    iotag = np.zeros((128, 512), dtype=np.float32)
    iotag[:, 0:384] = np.arange(384, dtype=np.float32)[None, :]
    iotag[:, 384:512] = np.tile(np.arange(8, dtype=np.float32), 16)[None, :]
    jcol = (np.arange(24, dtype=np.float32)[None, :] * 128.0
            + np.arange(128, dtype=np.float32)[:, None])
    jcol = np.ascontiguousarray(jcol)
    return ident, onesm, prope, rcos, rsin, triu, iotag, jcol


_prog_cache = {}


def _get_prog(n_sub=4):
    if n_sub not in _prog_cache:
        _prog_cache[n_sub] = build_program(n_sub)
    return _prog_cache[n_sub]


def kernel(L0_attn_a_x, L0_attn_a_Mq, L0_attn_a_Mk, L0_attn_a_Mv,
           L0_attn_b_wq, L0_attn_b_w_down, L0_attn_b_wk_up, L0_attn_b_wv_up,
           L0_attn_b_wo, L0_ffn_w_gate, L0_ffn_b_gate, L0_ffn_W1, L0_ffn_b1,
           L0_ffn_W2, L0_ffn_b2, _n_sub=4):
    f = lambda a: np.ascontiguousarray(np.asarray(a, dtype=np.float32))
    x = f(L0_attn_a_x)
    ident, onesm, prope, rcos, rsin, triu, iotag, jcol = _consts()
    b1l = np.ascontiguousarray(
        f(L0_ffn_b1).reshape(NE, 32, 128).transpose(2, 0, 1).reshape(128, NE * 32))
    shared = dict(
        Mq=f(L0_attn_a_Mq), Mk=f(L0_attn_a_Mk), Mv=f(L0_attn_a_Mv),
        wq=f(L0_attn_b_wq), wdown=f(L0_attn_b_w_down), wkup=f(L0_attn_b_wk_up),
        wvup=f(L0_attn_b_wv_up), wo=f(L0_attn_b_wo),
        wgate=f(L0_ffn_w_gate),
        bgate=np.ascontiguousarray(np.tile(f(L0_ffn_b_gate).reshape(1, NE), (128, 1))),
        W1=f(L0_ffn_W1), b1l=b1l, W2=f(L0_ffn_W2), b2=f(L0_ffn_b2),
        ident=ident, onesm=onesm, prope=prope, rcos=rcos, rsin=rsin,
        triu=triu, iotag=iotag, jcol=jcol,
    )
    nc = _get_prog(_n_sub)
    in_maps = [dict(shared, x=x[c]) for c in range(8)]
    res = run_bass_kernel_spmd(nc, in_maps, core_ids=list(range(8)))
    return np.stack([res.results[c]["y"] for c in range(8)], axis=0)


# version 08
# speedup vs baseline: 2.0495x; 1.2832x over previous
import numpy as np
from contextlib import ExitStack

import concourse.bass as bass
import concourse.mybir as mybir
from concourse.bass_utils import run_bass_kernel_spmd
from concourse.alu_op_type import AluOpType as ALU

F32 = mybir.dt.float32
BF16 = mybir.dt.bfloat16
AF = mybir.ActivationFunctionType
AX = mybir.AxisListType.X

T = 2048
D = 1024
NTC = T // 128
MLP = 4096
NE = 8
NH_B = 16
C_STAB = 20.0
C_STAB_A = 10.0
EPS = 1e-5
CAP = 384

SLAB_HALF = 65664


class Sched:
    def __init__(self):
        self.ops = []
        self.cnt = {"T": 0, "A": 0, "V": 0, "G": 0, "S": 0}

    def op(self, engine, fn, wait=(), inc=None, inc_k=1):
        if inc is not None:
            self.cnt[inc] += inc_k
        self.ops.append((engine, fn, list(wait), inc, inc_k))

    def wt(self, name):
        return (name, self.cnt[name])

    def barrier_vals(self):
        return [(k, v) for k, v in self.cnt.items() if v > 0]


def build_program(n_sub=4, dbg=False):
    nc = bass.Bass("TRN2", target_bir_lowering=False)

    xd = nc.dram_tensor("x", [T, D], F32, kind="ExternalInput").ap()
    Mqd = nc.dram_tensor("Mq", [D, 512], F32, kind="ExternalInput").ap()
    Mkd = nc.dram_tensor("Mk", [D, 512], F32, kind="ExternalInput").ap()
    Mvd = nc.dram_tensor("Mv", [D, D], F32, kind="ExternalInput").ap()
    wqd = nc.dram_tensor("wq", [D, D], F32, kind="ExternalInput").ap()
    wdownd = nc.dram_tensor("wdown", [D, 512], F32, kind="ExternalInput").ap()
    wkupd = nc.dram_tensor("wkup", [512, D], F32, kind="ExternalInput").ap()
    wvupd = nc.dram_tensor("wvup", [512, D], F32, kind="ExternalInput").ap()
    wod = nc.dram_tensor("wo", [D, D], F32, kind="ExternalInput").ap()
    wgated = nc.dram_tensor("wgate", [D, NE], F32, kind="ExternalInput").ap()
    bgated = nc.dram_tensor("bgate", [128, NE], F32, kind="ExternalInput").ap()
    W1d = nc.dram_tensor("W1", [NE, D, MLP], F32, kind="ExternalInput").ap()
    b1d = nc.dram_tensor("b1l", [128, NE * 32], F32, kind="ExternalInput").ap()
    W2d = nc.dram_tensor("W2", [NE, MLP, D], F32, kind="ExternalInput").ap()
    b2d = nc.dram_tensor("b2", [NE, D], F32, kind="ExternalInput").ap()
    identd = nc.dram_tensor("ident", [128, 128], F32, kind="ExternalInput").ap()
    onesd = nc.dram_tensor("onesm", [128, 128], F32, kind="ExternalInput").ap()
    proped = nc.dram_tensor("prope", [128, 128], F32, kind="ExternalInput").ap()
    triud = nc.dram_tensor("triu", [128, 128], F32, kind="ExternalInput").ap()
    iotagd = nc.dram_tensor("iotag", [128, 512], F32, kind="ExternalInput").ap()
    jcold = nc.dram_tensor("jcol", [128, 24], F32, kind="ExternalInput").ap()
    rcosd = nc.dram_tensor("rcos", [128, T], F32, kind="ExternalInput").ap()
    rsind = nc.dram_tensor("rsin", [128, T], F32, kind="ExternalInput").ap()
    yd = nc.dram_tensor("y", [T, D], F32, kind="ExternalOutput").ap()
    w2bfd = nc.dram_tensor("w2bf", [NE, MLP, D], BF16, kind="Internal").ap()

    xd_t = xd.rearrange("(tc p) d -> p tc d", p=128)
    yd_t = yd.rearrange("(tc p) d -> p tc d", p=128)

    es = ExitStack()

    def sb(name, shape, dtype):
        return es.enter_context(nc.sbuf_tensor(name, shape, dtype)).ap()

    x_res = sb("x_res", [128, NTC, D], F32)
    ident_bf = sb("ident_bf", [128, 128], BF16)
    ident_f = sb("ident_f", [128, 128], F32)
    ones_bf = sb("ones_bf", [128, 128], BF16)
    prope_sb = sb("prope_sb", [128, 128], BF16)
    b1_sb = sb("b1_sb", [128, NE * 32], F32)
    b2_sb = sb("b2_sb", [NE, D], BF16)
    wgate_sb = sb("wgate_sb", [128, NE, NE], BF16)
    bgate_bc = sb("bgate_bc", [128, NE], F32)
    xn_tile = sb("xn_tile", [128, 2, D], BF16)
    stat = sb("stat", [128, 16], F32)
    lnst = sb("lnst", [128, NTC, 2], F32)
    logit_sb = sb("logit_sb", [128, NTC, NE], F32)
    gb1 = sb("gb1", [128, NTC, NE], F32)
    gb2 = sb("gb2", [128, NTC, NE], F32)
    gb3 = sb("gb3", [128, NTC, NE], F32)
    gexp_all = sb("gexp_all", [128, NTC, NE], F32)
    gmx = sb("gmx", [128, NTC], F32)
    gsum = sb("gsum", [128, NTC], F32)
    grp = sb("grp", [128, NTC], F32)
    coef_sb = sb("coef_sb", [128, NTC, NE], F32)
    triu_bf = sb("triu_bf", [128, 128], BF16)
    iotag_sb = sb("iotag_sb", [128, 384], F32)
    jcol_sb = sb("jcol_sb", [128, 24], F32)
    oh_bf = sb("oh_bf", [128, 128], BF16)
    dscr = sb("dscr", [128, 64], F32)
    dsc2 = sb("dsc2", [128, 128], F32)
    destT_sb = sb("destT_sb", [128, 128], F32)

    slab = sb("slab", [128, SLAB_HALF], BF16)

    def carve_bf(off_bytes, shape):
        n = int(np.prod(shape))
        v = slab[:, off_bytes // 2: off_bytes // 2 + n]
        if len(shape) == 1:
            return v
        if len(shape) == 2:
            return v.rearrange("p (a b) -> p a b", a=shape[0], b=shape[1])
        return v.rearrange("p (a b c) -> p a b c", a=shape[0], b=shape[1], c=shape[2])

    def carve_f32_flat(off_bytes, n):
        v = slab[:, off_bytes // 2: off_bytes // 2 + 2 * n].bitcast(F32)
        return v

    A_xntc = carve_bf(0, [2, 8, 128])
    A_xnTg = carve_bf(4096, [8, 512])
    A_qfg = carve_bf(12288, [8, 512])
    A_kfg = carve_bf(20480, [8, 512])
    A_aring = carve_bf(65536, [2, T])
    A_Mv = carve_bf(32768, [8, 1024])
    A_Mq = carve_bf(49152, [8, 512])
    A_Mk = carve_bf(57344, [8, 512])
    A_vf = carve_bf(98304, [16, D])
    F_xn_tm = carve_bf(0, [16, 1024])
    F_xnT = carve_bf(32768, [8, 128])
    F_w1 = carve_bf(34816, [2, 8, 1024])
    F_w2c = carve_bf(67584, [6, 1024])
    F_xgT = carve_bf(79872, [8, CAP])
    F_hgT = carve_bf(86016, [2, 8, CAP])
    F_yg = carve_bf(98304, [3, 1024])
    F_Gg = carve_bf(104448, [16, CAP])
    F_GsR = carve_bf(116736, [2, 3, 512])
    F_dbc = carve_f32_flat(122880, 2048).rearrange(
        "p (a b) -> p a b", a=16, b=128)
    F_coefT = carve_bf(86016, [T])
    B_cos = carve_bf(0, [1, T])
    B_sin = carve_bf(4096, [1, T])
    B_kr = carve_bf(8192, [8, T])
    B_v = carve_bf(40960, [16, 16, 65])
    B_latT = carve_bf(74240, [4, T])
    B_wd = carve_bf(90624, [8, 512])
    B_wku = carve_bf(98816, [4, 1024])
    B_wv = carve_bf(90624, [4, 1024])
    B_xn3T = carve_bf(107520, [8, 512])
    B_tmp = carve_bf(115712, [2, 512])
    B_wq = carve_bf(74240, [8, 1024])
    B_wo = carve_bf(90624, [8, 1024])
    B_qT = carve_bf(115712, [8, 512])
    B_At = carve_bf(123904, [2, 256])
    B_aoT = carve_bf(124928, [8, 256])
    B_rbx = carve_bf(129024, [64 * 4])
    rrow_sb = carve_bf(129536, [256])

    PS = es.enter_context(nc.psum_tensor("ps_all", [128, 4096], F32)).ap()

    def bank(i, n=1):
        return PS[:, i * 512:(i + n) * 512]

    def bank_bf(i):
        return PS.bitcast(BF16)[:, i * 1024:(i + 1) * 1024]

    sems = {k: es.enter_context(nc.semaphore("sem_" + k)) for k in ["T", "A", "V", "G", "S"]}

    sch = Sched()
    OP, WT = sch.op, sch.wt

    def dma_g(dst, src, wait=()):
        OP("G", lambda e, d=dst, s=src: e.dma_start(out=d, in_=s), wait=wait, inc="G", inc_k=16)

    def dma_s(dst, src, wait=()):
        OP("S", lambda e, d=dst, s=src: e.dma_start(out=d, in_=s), wait=wait, inc="S", inc_k=16)

    def mm_group(out, pairs, wait=()):
        n = len(pairs)
        for i, (l, r) in enumerate(pairs):
            OP("T", lambda e, o=out, l=l, r=r, st=(i == 0), sp=(i == n - 1):
               e.matmul(o, l, r, start=st, stop=sp),
               wait=(wait if i == 0 else ()), inc=("T" if i == n - 1 else None))

    def act(fn, wait=()):
        OP("A", fn, wait=wait, inc="A")

    def dve(fn, wait=()):
        OP("V", fn, wait=wait, inc="V")

    def barrier():
        vals = sch.barrier_vals()
        for eng in ["T", "A", "V", "G", "S"]:
            w = [(k, v) for k, v in vals if k != eng]
            OP(eng, lambda e: None, wait=w)

    def ring_guard(sem):
        return (sem, max(0, sch.cnt[sem] - 1))

    for i in range(4):
        dma_s(x_res[:, 4 * i:4 * i + 4, :], xd_t[:, 4 * i:4 * i + 4, :])
    x_loaded = WT("S")
    dve(lambda e: e.memset(stat[:, 14:15], EPS))
    dve(lambda e: e.memset(stat[:, 15:16], -C_STAB))
    dve(lambda e: e.memset(stat[:, 8:9], -C_STAB_A))
    dma_g(ident_bf, identd)
    dma_g(ident_f, identd)
    dma_g(ones_bf, onesd)
    dma_g(triu_bf, triud)
    dma_g(iotag_sb, iotagd[:, 0:384])
    dma_g(jcol_sb, jcold)
    dma_g(prope_sb, proped)
    dma_g(b1_sb, b1d)
    dma_g(b2_sb, b2d)
    dma_g(wgate_sb, wgated.rearrange("(c p) e -> p c e", p=128))
    dma_g(bgate_bc, bgated)
    dma_g(A_Mq, Mqd.rearrange("(c p) f -> p c f", p=128))
    dma_g(A_Mk, Mkd.rearrange("(c p) f -> p c f", p=128))
    dma_g(A_Mv, Mvd.rearrange("(c p) f -> p c f", p=128))
    w_loads0 = WT("G")

    cstage = [carve_bf(73728, [4, 1024]), carve_bf(81920, [4, 1024])]
    _wrS = {}
    for e_c in range(NE):
        w2v = W2d[e_c].rearrange("(c p) d -> p c d", p=128)
        w2bv = w2bfd[e_c].rearrange("(c p) d -> p c d", p=128)
        for kc in range(8):
            idx = e_c * 8 + kc
            r = idx % 2
            dma_g(cstage[r], w2v[:, 4 * kc:4 * kc + 4, :],
                  wait=([("S", _wrS[idx - 2])] if idx >= 2 else []))
            cast_G = WT("G")
            dma_s(w2bv[:, 4 * kc:4 * kc + 4, :], cstage[r], wait=[cast_G])
            _wrS[idx] = sch.cnt["S"]

    def emit_ln(tc_list, dst_of, save_stats=False, use_saved=False, after_tile=None,
                apply_dst=None):
        for j, tc in enumerate(tc_list):
            ring = tc % 2
            ap_dst = apply_dst(tc) if apply_dst is not None else xn_tile[:, ring, :]
            if use_saved:
                rstd = lnst[:, tc, 0:1]
                nmr = lnst[:, tc, 1:2]
            else:
                rstd = stat[:, 0:1]
                nmr = stat[:, 1:2]
                s1 = stat[:, 2:3]
                sq = stat[:, 3:4]
                m = stat[:, 4:5]
                var = stat[:, 5:6]
                dve(lambda e, tc=tc, s1=s1: e.reduce_sum(s1, x_res[:, tc, :], axis=AX),
                    wait=([x_loaded] if j == 0 else ()))
                dve(lambda e, tc=tc, ring=ring, sq=sq: e.scalar_tensor_tensor(
                    xn_tile[:, ring, :], x_res[:, tc, :], 1.0, x_res[:, tc, :],
                    op0=ALU.bypass, op1=ALU.mult, accum_out=sq))
                dve(lambda e, s1=s1, m=m: e.tensor_scalar(m, s1, 1.0 / D, None, op0=ALU.mult))
                dve(lambda e, m=m, var=var: e.tensor_scalar(var, m, m[:, 0:1], None,
                                                            op0=ALU.mult))
                dve(lambda e, sq=sq, var=var: e.scalar_tensor_tensor(
                    var, sq, 1.0 / D, var, op0=ALU.mult, op1=ALU.subtract))
                act(lambda e, var=var: e.activation(stat[:, 6:7], var, AF.Ln,
                                                    bias=stat[:, 14:15], scale=1.0),
                    wait=[WT("V")])
                act(lambda e, rstd=rstd: e.activation(rstd, stat[:, 6:7], AF.Exp, bias=0.0, scale=-0.5))
                dve(lambda e, m=m, rstd=rstd, nmr=nmr: e.tensor_scalar(
                    nmr, m, rstd[:, 0:1], -1.0, op0=ALU.mult, op1=ALU.mult), wait=[WT("A")])
                if save_stats:
                    dve(lambda e, tc=tc, rstd=rstd: e.tensor_copy(lnst[:, tc, 0:1], rstd))
                    dve(lambda e, tc=tc, nmr=nmr: e.tensor_copy(lnst[:, tc, 1:2], nmr))
            act(lambda e, tc=tc, ap_dst=ap_dst, rstd=rstd, nmr=nmr: e.activation(
                ap_dst, x_res[:, tc, :], AF.Identity,
                bias=nmr[:, 0:1], scale=rstd[:, 0:1]),
                wait=[WT("V")])
            a_apply = WT("A")
            dst, co = dst_of(tc)
            for dmc in range(8):
                OP("T", lambda e, ap_dst=ap_dst, dmc=dmc: e.transpose(
                    bank_bf(7)[:, dmc * 128:(dmc + 1) * 128],
                    ap_dst[:, dmc * 128:(dmc + 1) * 128], ident_bf),
                   wait=([a_apply, WT("V")] if dmc == 0 else ()),
                   inc=("T" if dmc == 7 else None))
            tdone = WT("T")
            dve(lambda e, co=co, dst=dst: e.tensor_copy(
                dst[:, :, co:co + 128],
                bank_bf(7).rearrange("p (c f) -> p c f", c=8)),
                wait=[tdone])
            if after_tile is not None:
                after_tile(tc)

    def emit_vf(tc):
        ring = tc % 2
        cp_V = WT("V")
        for half in range(2):
            rb = (tc * 2 + half) % 2
            w = [cp_V, w_loads0] if half == 0 else []
            w.append(ring_guard("A"))
            mm_group(bank(rb),
                     [(A_xntc[:, ring, dmc, :], A_Mv[:, dmc, half * 512:(half + 1) * 512])
                      for dmc in range(8)], wait=w)
            mmil = WT("T")
            act(lambda e, tc=tc, half=half, rb=rb: e.activation(
                A_vf[:, tc, half * 512:(half + 1) * 512], bank(rb), AF.Copy,
                bias=0.0, scale=1.0), wait=[mmil])

    emit_ln(list(range(NTC)), lambda tc: (A_xntc[:, tc % 2], 0),
            save_stats=True, after_tile=emit_vf)
    p1_done_A = WT("A")
    p1_done_T = WT("T")

    if n_sub >= 1:
        mile_sum = 0
        for gp in range(4):
            emit_ln([4 * gp + j for j in range(4)],
                    lambda tc: (A_xnTg, (tc % 4) * 128), use_saved=True)
            lng_V = WT("V")
            for which, Msb, dstg, scale in ((0, A_Mq, A_qfg, 0.125),
                                            (1, A_Mk, A_kfg, 1.0)):
                for s in range(8):
                    rb = s % 2
                    w = [lng_V] if (which == 0 and s == 0) else []
                    w.append(ring_guard("A"))
                    mm_group(bank(rb)[0:64, 0:512],
                             [(Msb[:, dmc, 64 * s:64 * s + 64], A_xnTg[:, dmc, :])
                              for dmc in range(8)], wait=w)
                    mmil = WT("T")
                    act(lambda e, s=s, rb=rb, dstg=dstg, scale=scale: e.activation(
                        dstg[0:64, s, :], bank(rb)[0:64, 0:512], AF.Copy,
                        bias=0.0, scale=scale), wait=[mmil])
            qk_A = WT("A")
            for gl in range(2):
              g = 2 * gp + gl
              if True:
                mile_exp_h = {}
                mile_norm = {}

              def emit_av(j):
                  jr = j % 2
                  for rc in range(2):
                      for h2 in range(8):
                          bi = 4 + rc * 2 + h2 // 4
                          off = (h2 % 4) * 128
                          st = (j == 0) and (h2 % 4 == 0)
                          sp = (j == 15) and (h2 % 4 == 3)
                          w = [("V", mile_norm[j])] if (rc == 0 and h2 == 0) else ()
                          OP("T", lambda e, j=j, rc=rc, h2=h2, bi=bi, off=off, st=st,
                             sp=sp, jr=jr: e.matmul(
                              bank(bi)[:, off:off + 128],
                              A_aring[:, jr, 256 * h2 + 128 * rc: 256 * h2 + 128 * rc + 128],
                              A_vf[:, 2 * h2 + (j % 2), 128 * (j // 2):128 * (j // 2) + 128],
                              start=st, stop=sp),
                             wait=w, inc=("T" if (rc == 1 and h2 == 7) else None))

              for m in range(16):
                  ring = m % 2
                  tS = m // 2
                  for s in range(8):
                      bi = s // 2
                      off = (s % 2) * 256
                      w = []
                      if s == 0 or s == 4:
                          half = s // 4
                          if m == 0:
                              w.append(("A", sch.cnt["A"]))
                          else:
                              w.append(("A", mile_exp_h[(m - 1, half)]))
                      OP("T", lambda e, m=m, s=s, bi=bi, off=off, tS=tS, gl=gl: e.matmul(
                          bank(bi)[:, off:off + 256],
                          A_kfg[0:64, tS, 256 * gl + 128 * (m % 2):256 * gl + 128 * (m % 2) + 128],
                          A_qfg[0:64, s, 256 * gl:256 * gl + 256],
                          start=True, stop=True),
                         wait=w, inc=("T" if (s == 3 or s == 7) else None))
                      if s == 3 or s == 7:
                          half = s // 4
                          schalf_T = WT("T")
                          wexp = [schalf_T]
                          if half == 0 and m >= 1:
                              wexp.append(("V", mile_sum))
                          act(lambda e, ring=ring, half=half: e.activation(
                              A_aring[:, ring, half * 1024:(half + 1) * 1024],
                              PS[:, half * 1024:(half + 1) * 1024], AF.Exp,
                              bias=stat[:, 8:9], scale=1.0,
                              accum_out=stat[:, 9 + half:10 + half]),
                              wait=wexp)
                          mile_exp_h[(m, half)] = sch.cnt["A"]
                  dve(lambda e: e.tensor_scalar(stat[:, 11:12], stat[:, 9:10],
                                                stat[:, 10:11], None, op0=ALU.add),
                      wait=[("A", mile_exp_h[(m, 1)])])
                  mile_sum = sch.cnt["V"]
                  dve(lambda e: e.reciprocal(stat[:, 12:13], stat[:, 11:12]))
                  dve(lambda e, ring=ring: e.tensor_scalar(
                      A_aring[:, ring, :], A_aring[:, ring, :], stat[:, 12:13], None,
                      op0=ALU.mult))
                  mile_norm[m] = sch.cnt["V"]
                  if m >= 1:
                      emit_av(m - 1)
              emit_av(15)
              av_T = WT("T")
              for rc in range(2):
                  dve(lambda e, g=g, rc=rc: e.tensor_tensor(
                      x_res[:, 2 * g + rc, :], bank(4 + rc * 2, 2),
                      x_res[:, 2 * g + rc, :], op=ALU.add),
                      wait=([av_T] if rc == 0 else ()))
              evg_V = WT("V")
              OP("T", lambda e: None, wait=[evg_V])
        barrier()

    def emit_ffn():
        entry_T = sch.cnt["T"]
        def gate_after(tc):
            gbk = 6 + tc % 2
            mm_group(bank(gbk)[:, 0:8],
                     [(F_xnT[:, dmc, :], wgate_sb[:, dmc, :]) for dmc in range(8)],
                     wait=[WT("V")])
            gm_T = WT("T")
            dve(lambda e, tc=tc, gbk=gbk: e.tensor_tensor(
                logit_sb[:, tc, :], bank(gbk)[:, 0:8], bgate_bc, op=ALU.add),
                wait=[gm_T])

        emit_ln(list(range(NTC)), lambda tc: (F_xnT, 0),
                apply_dst=lambda tc: F_xn_tm[:, tc, :], after_tile=gate_after)
        def bc16(v):
            return v.rearrange("p (a o) -> p a o", o=1).broadcast_to((128, NTC, NE))
        dve(lambda e: e.reduce_max(gmx, logit_sb, axis=AX, negate=True))
        dve(lambda e: e.tensor_tensor(gb1, logit_sb, bc16(gmx), op=ALU.add))
        gb1_V = WT("V")
        act(lambda e: e.activation(
            gexp_all.rearrange("p a b -> p (a b)"), gb1.rearrange("p a b -> p (a b)"),
            AF.Exp, bias=0.0, scale=1.0), wait=[gb1_V])
        exps_A = WT("A")
        dve(lambda e: e.tensor_reduce(gsum, gexp_all, axis=AX, op=ALU.add),
            wait=[exps_A])
        dve(lambda e: e.reciprocal(grp, gsum))
        dve(lambda e: e.tensor_scalar(gb2, gb1, 0.0, None, op0=ALU.is_equal))
        dve(lambda e: e.memset(gb3.rearrange("p a b -> p (a b)"), 0.0))
        dve(lambda e: e.tensor_copy(gb3[:, :, 1:8], gb2[:, :, 0:7]))
        dve(lambda e: e.tensor_copy(gb1, gb3))
        dve(lambda e: e.tensor_tensor(gb1[:, :, 2:8], gb3[:, :, 2:8], gb3[:, :, 0:6],
                                      op=ALU.add))
        dve(lambda e: e.tensor_copy(gb3, gb1))
        dve(lambda e: e.tensor_tensor(gb3[:, :, 4:8], gb1[:, :, 4:8], gb1[:, :, 0:4],
                                      op=ALU.add))
        dve(lambda e: e.tensor_scalar(gb1, gb3, 0.0, None, op0=ALU.is_equal))
        dve(lambda e: e.tensor_tensor(gb2, gb2, gb1, op=ALU.mult))
        dve(lambda e: e.tensor_tensor(coef_sb, gb2, bc16(grp), op=ALU.mult))
        cf_V = WT("V")

        gb2v = gb2.rearrange("p a b -> p (a b)")
        gb1v = gb1.rearrange("p a b -> p (a b)")
        gb3v = gb3.rearrange("p a b -> p (a b)")
        dve(lambda e: e.tensor_copy(oh_bf, gb2v))
        dve(lambda e: e.tensor_copy(dsc2[:, 0:8], iotag_sb[:, 0:8]))
        for w_ in (8, 16, 32, 64):
            dve(lambda e, w_=w_: e.tensor_copy(dsc2[:, w_:2 * w_], dsc2[:, 0:w_]))
        dve(lambda e: e.tensor_tensor(dsc2, dsc2, gb2v, op=ALU.mult))
        dve(lambda e: e.tensor_reduce(
            dscr[:, 0:16], dsc2.rearrange("p (a b) -> p a b", a=NTC),
            axis=AX, op=ALU.add))
        ohb_V = WT("V")
        OP("T", lambda e: e.matmul(bank(0)[:, 0:128], triu_bf, oh_bf,
                                   start=True, stop=True), wait=[ohb_V], inc="T")
        OP("T", lambda e: e.matmul(bank(1)[0:1, 0:128], ones_bf[:, 0:1], oh_bf,
                                   start=True, stop=True), inc="T")
        pref_T = WT("T")
        dve(lambda e: e.memset(gb1v[0:1, 0:8], 0.0), wait=[pref_T])
        dve(lambda e: e.tensor_copy(gb1v[0:1, 8:128], bank(1)[0:1, 0:120]))
        bufs = [gb1v, gb3v]
        cur = 0
        for k in (8, 16, 32, 64):
            a, b = bufs[cur], bufs[1 - cur]
            dve(lambda e, a=a, b=b, k=k: e.tensor_copy(
                b[0:1, 0:k], a[0:1, 0:k]))
            dve(lambda e, a=a, b=b, k=k: e.tensor_tensor(
                b[0:1, k:128], a[0:1, k:128],
                a[0:1, 0:128 - k], op=ALU.add))
            cur = 1 - cur
        scan_buf = bufs[cur]
        dve(lambda e: e.memset(dsc2[0:1, 0:128], 1.0))
        scan_V = WT("V")
        OP("T", lambda e, a=scan_buf: e.matmul(
            bank(1)[:, 0:128], dsc2[0:1, 0:128],
            a[0:1, 0:128], start=True, stop=True),
           wait=[scan_V], inc="T")
        base_T = WT("T")
        dve(lambda e: e.tensor_tensor(dsc2, bank(0)[:, 0:128], gb2v, op=ALU.mult),
            wait=[base_T])
        dve(lambda e: e.tensor_reduce(
            dscr[:, 16:32], dsc2.rearrange("p (a b) -> p a b", a=NTC),
            axis=AX, op=ALU.add))
        dve(lambda e: e.tensor_tensor(dsc2, bank(1)[:, 0:128], gb2v, op=ALU.mult))
        dve(lambda e: e.tensor_reduce(
            dscr[:, 48:64], dsc2.rearrange("p (a b) -> p a b", a=NTC),
            axis=AX, op=ALU.add))
        dve(lambda e: e.tensor_tensor(dscr[:, 16:32], dscr[:, 16:32], dscr[:, 48:64],
                                      op=ALU.add))
        dve(lambda e: e.tensor_scalar(dscr[:, 16:32], dscr[:, 16:32], float(CAP - 1),
                                      None, op0=ALU.min))
        dve(lambda e: e.scalar_tensor_tensor(dscr[:, 32:48], dscr[:, 0:16], float(CAP),
                                             dscr[:, 16:32], op0=ALU.mult, op1=ALU.add))
        dve(lambda e: e.memset(dsc2[0:1, 0:128], 1.0))
        dest_V = WT("V")
        OP("T", lambda e: e.transpose(bank(6)[0:16, 0:128], dscr[:, 32:48], ident_f),
           wait=[dest_V], inc="T")
        dtr_T = WT("T")
        dve(lambda e: e.tensor_copy(destT_sb[0:16, 0:128], bank(6)[0:16, 0:128]),
            wait=[dtr_T])
        dtc_V = WT("V")
        dma_s(F_dbc[0:1].rearrange("o a b -> o (a b)"), destT_sb[0:16, 0:128],
              wait=[dtc_V])
        drow_S = WT("S")
        for q in range(4):
            OP("T", lambda e, q=q: e.matmul(
                bank(2 + q)[:, 0:512], dsc2[0:1, 0:128],
                F_dbc[0:1].rearrange("o a b -> o (a b)")[:, q * 512:(q + 1) * 512],
                start=True, stop=True),
               wait=([drow_S] if q == 0 else ()), inc="T")
            qT = WT("T")
            dve(lambda e, q=q: e.tensor_copy(
                F_dbc[:, 4 * q:4 * q + 4, :].rearrange("p a b -> p (a b)"),
                bank(2 + q)[:, 0:512]), wait=[qT])
        dbc_V = WT("V")

        for tc in range(NTC):
            OP("T", lambda e, tc=tc: e.transpose(bank(7)[0:8, 0:128], coef_sb[:, tc, :],
                                                 ident_f),
               wait=([cf_V] if tc == 0 else [("V", sch.cnt["V"])]), inc="T")
            dve(lambda e, tc=tc: e.tensor_copy(
                F_coefT[0:8, tc * 128:(tc + 1) * 128], bank(7)[0:8, 0:128]),
                wait=[WT("T")])
        for tc in range(NTC):
            w = [("V", sch.cnt["V"])]
            for hh in range(2):
                OP("T", lambda e, tc=tc, hh=hh: e.matmul(
                    bank(4 + hh), F_coefT[0:8, tc * 128:(tc + 1) * 128],
                    b2_sb[:, hh * 512:(hh + 1) * 512], start=True, stop=True),
                   wait=(w if hh == 0 else ()), inc="T")
            b2_T = WT("T")
            dve(lambda e, tc=tc: e.tensor_tensor(
                x_res[:, tc, :], bank(4, 2), x_res[:, tc, :], op=ALU.add), wait=[b2_T])
        b2_done_T = sch.cnt["T"]
        b2_done_V = WT("V")

        w1_free = [entry_T, entry_T]
        hgT_free = [b2_done_T, b2_done_T]
        w2_lastread = [entry_T] * 6
        w2_loaded = {}
        ge_T = 0
        xg_A = 0
        yg_A = 0
        w2_order = [(ee, mm, cc) for ee in range(NE) for mm in range(4)
                    for cc in range(8)]
        w2_next = [0]

        def pump_w2(upto):
            while w2_next[0] < min(len(w2_order), upto):
                gci = w2_next[0]
                e2, mg2, mc82 = w2_order[gci]
                ci = gci % 6
                dma_s(F_w2c[:, ci, :],
                      w2bfd[e2][mg2 * 1024 + mc82 * 128: mg2 * 1024 + (mc82 + 1) * 128, :],
                      wait=[("T", w2_lastread[ci])])
                w2_loaded[gci] = WT("S")
                w2_next[0] += 1

        pump_w2(6)
        for e_i in range(NE):
            dve(lambda e, e_i=e_i: e.tensor_scalar(
                dscr[:, 48:64], dscr[:, 32:48], float(-CAP * e_i), None, op0=ALU.add),
                wait=[("T", ge_T), dest_V] if e_i > 0 else [dbc_V, dest_V])
            for tc in range(NTC):
                dve(lambda e, tc=tc: e.tensor_scalar(
                    F_Gg[:, tc, :], iotag_sb[:, 0:CAP], dscr[:, 48 + tc:49 + tc],
                    None, op0=ALU.is_equal))
            gg_V = WT("V")
            for dmc in range(8):
                for tc in range(NTC):
                    w = []
                    if tc == 0:
                        w.append(gg_V)
                        if dmc == 0:
                            w.append(("A", sch.cnt["A"]))
                        elif dmc >= 2:
                            w.append(("A", xg_A))
                    OP("T", lambda e, dmc=dmc, tc=tc: e.matmul(
                        bank(dmc % 2)[:, 0:CAP],
                        F_xn_tm[:, tc, dmc * 128:(dmc + 1) * 128], F_Gg[:, tc, :],
                        start=(tc == 0), stop=(tc == NTC - 1)),
                       wait=w, inc=("T" if tc == NTC - 1 else None))
                g_T = WT("T")
                act(lambda e, dmc=dmc: e.activation(
                    F_xgT[:, dmc, :], bank(dmc % 2)[:, 0:CAP], AF.Copy,
                    bias=0.0, scale=1.0), wait=[g_T])
                xg_A = sch.cnt["A"]
            ge_T = sch.cnt["T"]

            hg_A = {}
            y_done = {}

            def emit_h(mg, e_i=e_i):
                slot = mg % 2
                for dmc in range(8):
                    dma_g(F_w1[:, slot, dmc, :],
                          W1d[e_i][dmc * 128:(dmc + 1) * 128, mg * 1024:(mg + 1) * 1024],
                          wait=([("T", w1_free[slot])] if dmc == 0 else ()))
                w1_S = WT("G")
                for mc8 in range(8):
                    w = []
                    if mc8 == 0:
                        w = [w1_S, ("A", xg_A)]
                    elif mc8 >= 2:
                        w = [("A", hg_A[(mg, mc8 - 2)])]
                    for dmc in range(8):
                        OP("T", lambda e, mc8=mc8, dmc=dmc, slot=slot: e.matmul(
                            bank(mc8 % 2)[:, 0:CAP],
                            F_w1[:, slot, dmc, mc8 * 128:(mc8 + 1) * 128],
                            F_xgT[:, dmc, :],
                            start=(dmc == 0), stop=(dmc == 7)),
                           wait=(w if dmc == 0 else ()),
                           inc=("T" if dmc == 7 else None))
                    hm_T = WT("T")
                    wv = [hm_T]
                    if mc8 == 0:
                        wv.append(("T", hgT_free[slot]))
                    act(lambda e, e_i=e_i, mg=mg, mc8=mc8, slot=slot: e.activation(
                        F_hgT[:, slot, mc8, :], bank(mc8 % 2)[:, 0:CAP], AF.Relu,
                        bias=b1_sb[:, e_i * 32 + mg * 8 + mc8: e_i * 32 + mg * 8 + mc8 + 1],
                        scale=1.0), wait=wv)
                    hg_A[(mg, mc8)] = sch.cnt["A"]
                w1_free[slot] = sch.cnt["T"]

            def emit_y(mg, e_i=e_i):
                slot = mg % 2
                for mc8 in range(8):
                    gci = (e_i * 4 + mg) * 8 + mc8
                    ci = gci % 6
                    for jt in range(3):
                        for hh in range(2):
                            w = []
                            if jt == 0 and hh == 0:
                                w.append(w2_loaded[gci])
                                if mc8 == 0:
                                    w.append(("A", hg_A[(mg, 7)]))
                                    if mg == 0:
                                        w.append(("A", sch.cnt["A"]))
                                        w.append(("V", sch.cnt["V"]))
                            OP("T", lambda e, mg=mg, mc8=mc8, jt=jt, hh=hh, ci=ci,
                               slot=slot: e.matmul(
                                bank(2 + 2 * jt + hh),
                                F_hgT[:, slot, mc8, jt * 128:(jt + 1) * 128],
                                F_w2c[:, ci, hh * 512:(hh + 1) * 512],
                                start=(mg == 0 and mc8 == 0),
                                stop=(mg == 3 and mc8 == 7)),
                               wait=w, inc=("T" if (jt == 2 and hh == 1) else None))
                    w2_lastread[ci] = sch.cnt["T"]
                    pump_w2(gci + 7)
                y_done[mg] = sch.cnt["T"]
                hgT_free[slot] = sch.cnt["T"]

            emit_h(0)
            emit_h(1)
            emit_y(0)
            emit_h(2)
            emit_y(1)
            emit_h(3)
            emit_y(2)
            emit_y(3)
            yl_T = WT("T")
            for jt in range(3):
                for hh in range(2):
                    act(lambda e, jt=jt, hh=hh: e.activation(
                        F_yg[:, jt, hh * 512:(hh + 1) * 512],
                        bank(2 + 2 * jt + hh), AF.Copy, bias=0.0, scale=1.0),
                        wait=([yl_T] if (jt == 0 and hh == 0) else ()))
            yg_A = sch.cnt["A"]

            gs_V = {}
            ps_A = {}
            tt_V = {}
            sc_T = {}
            for tc in range(NTC):
                q = tc // 4
                if tc % 4 == 0:
                    for jc in range(3):
                        wv = [("T", sc_T[q - 2])] if q >= 2 else []
                        dve(lambda e, q=q, jc=jc, e_i=e_i: e.tensor_scalar(
                            F_GsR[:, q % 2, jc, :],
                            F_dbc[:, 4 * q:4 * q + 4, :].rearrange("p a b -> p (a b)"),
                            jcol_sb[:, 3 * e_i + jc:3 * e_i + jc + 1],
                            None, op0=ALU.is_equal), wait=wv)
                    gs_V[q] = WT("V")
                par = tc % 2
                for hh in range(2):
                    w = []
                    if hh == 0:
                        w.append(gs_V[q])
                        if tc == 0:
                            w.append(("A", yg_A))
                        if tc >= 2:
                            w.append(("A", ps_A[(tc - 2, 1)]))
                    for jc in range(3):
                        OP("T", lambda e, tc=tc, hh=hh, jc=jc, q=q, par=par: e.matmul(
                            bank(2 * par + hh),
                            F_GsR[:, q % 2, jc, (tc % 4) * 128:(tc % 4 + 1) * 128],
                            F_yg[:, jc, hh * 512:(hh + 1) * 512],
                            start=(jc == 0), stop=(jc == 2)),
                           wait=(w if jc == 0 else ()),
                           inc=("T" if jc == 2 else None))
                    s_T = WT("T")
                    wp = [s_T]
                    if tc >= 2:
                        wp.append(("V", tt_V[tc - 2]))
                    act(lambda e, tc=tc, hh=hh, par=par, e_i=e_i: e.activation(
                        bank(4 + 2 * par + hh), bank(2 * par + hh), AF.Copy,
                        bias=0.0, scale=coef_sb[:, tc, e_i:e_i + 1]), wait=wp)
                    ps_A[(tc, hh)] = sch.cnt["A"]
                sc_T[tc // 4] = sch.cnt["T"]
                dve(lambda e, tc=tc, par=par: e.tensor_tensor(
                    x_res[:, tc, :], bank(4 + 2 * par, 2), x_res[:, tc, :],
                    op=ALU.add), wait=[("A", ps_A[(tc, 1)])])
                tt_V[tc] = sch.cnt["V"]

    if n_sub >= 2:
        emit_ffn()
        barrier()

    if n_sub >= 3:
        dma_g(B_cos[:, 0, :], rcosd, wait=[("T", sch.cnt["T"])])
        dma_g(B_sin[:, 0, :], rsind)
        dma_g(B_wd, wdownd.rearrange("(c p) f -> p c f", p=128))
        dma_g(B_wku, wkupd.rearrange("(c p) f -> p c f", p=128))
        b1_loads = WT("G")
        for tl in range(4):
            emit_ln([4 * tl + j for j in range(4)],
                    lambda tc: (B_xn3T, (tc % 4) * 128), save_stats=True)
            ln3_V = WT("V")
            for lfc in range(4):
                rb = lfc % 2
                w = [b1_loads, ln3_V] if lfc == 0 else []
                w.append(ring_guard("A"))
                mm_group(bank(rb),
                         [(B_wd[:, dmc, lfc * 128:(lfc + 1) * 128],
                           B_xn3T[:, dmc, :]) for dmc in range(8)], wait=w)
                lm_T = WT("T")
                act(lambda e, tl=tl, lfc=lfc, rb=rb: e.activation(
                    B_latT[:, lfc, tl * 512:(tl + 1) * 512], bank(rb), AF.Copy,
                    bias=0.0, scale=1.0), wait=[lm_T])
        latT_A = WT("A")
        for tt in range(4):
            for kfc in range(8):
                rb = (tt * 8 + kfc) % 2
                w = [latT_A] if (tt == 0 and kfc == 0) else []
                w.append(ring_guard("A"))
                mm_group(bank(rb),
                         [(B_wku[:, lfc, kfc * 128:(kfc + 1) * 128],
                           B_latT[:, lfc, tt * 512:(tt + 1) * 512]) for lfc in range(4)],
                         wait=w)
                km_T = WT("T")
                act(lambda e, tt=tt, kfc=kfc, rb=rb: e.activation(
                    B_kr[:, kfc, tt * 512:(tt + 1) * 512], bank(rb), AF.Copy,
                    bias=0.0, scale=1.0), wait=[km_T])
        kT_done_A = WT("A")
        dma_g(B_wv, wvupd.rearrange("(c p) f -> p c f", p=128), wait=[WT("T")])
        wv_G = WT("G")
        for tc in range(NTC):
            for hh in range(2):
                rb = (tc * 2 + hh) % 2
                w = [wv_G, kT_done_A] if (tc == 0 and hh == 0) else []
                w.append(ring_guard("A"))
                mm_group(bank(rb),
                         [(B_latT[:, lfc, tc * 128:(tc + 1) * 128],
                           B_wv[:, lfc, hh * 512:(hh + 1) * 512]) for lfc in range(4)],
                         wait=w)
                vm_T = WT("T")
                act(lambda e, tc=tc, hh=hh, rb=rb: e.activation(
                    B_v[:, tc, 8 * hh:8 * hh + 8, 0:64],
                    bank(rb), AF.Copy, bias=0.0, scale=1.0), wait=[vm_T])
        dve(lambda e: e.memset(B_v[:, :, :, 64:65], 1.0))
        v_done_T = WT("T")
        v_done_A = WT("A")
        v_ones_V = WT("V")
        for kfc in range(8):
            for tt in range(4):
                rb = 2 + (kfc * 4 + tt) % 2
                w = [kT_done_A, b1_loads] if (kfc == 0 and tt == 0) else []
                w.append(("V", max(0, sch.cnt["V"] - 2)))
                OP("T", lambda e, kfc=kfc, tt=tt, rb=rb: e.matmul(
                    bank(rb), prope_sb, B_kr[:, kfc, tt * 512:(tt + 1) * 512],
                    start=True, stop=True), wait=w, inc="T")
                sw_T = WT("T")
                dve(lambda e, kfc=kfc, tt=tt: e.tensor_tensor(
                    B_tmp[:, 0, :], B_kr[:, kfc, tt * 512:(tt + 1) * 512],
                    B_cos[:, 0, tt * 512:(tt + 1) * 512], op=ALU.mult), wait=[sw_T])
                dve(lambda e, tt=tt, rb=rb: e.tensor_tensor(
                    B_tmp[:, 1, :], bank(rb),
                    B_sin[:, 0, tt * 512:(tt + 1) * 512], op=ALU.mult))
                dve(lambda e, kfc=kfc, tt=tt: e.tensor_tensor(
                    B_kr[:, kfc, tt * 512:(tt + 1) * 512], B_tmp[:, 0, :],
                    B_tmp[:, 1, :], op=ALU.add))
        krope_V = WT("V")
        dma_g(B_wq, wqd.rearrange("(c p) f -> p c f", p=128), wait=[v_done_T])
        dma_g(B_wo, wod.rearrange("(c p) f -> p c f", p=128), wait=[krope_V])
        b2w_G = WT("G")

        for tl in range(4):
            emit_ln([4 * tl + j for j in range(4)],
                    lambda tc: (B_xn3T, (tc % 4) * 128), use_saved=True)
            ln3b_V = WT("V")
            for qfc in range(8):
                rb = qfc % 2
                w = [b2w_G, ln3b_V] if qfc == 0 else []
                w.append(ring_guard("A"))
                mm_group(bank(rb),
                         [(B_wq[:, dmc, qfc * 128:(qfc + 1) * 128],
                           B_xn3T[:, dmc, :]) for dmc in range(8)], wait=w)
                qm_T = WT("T")
                act(lambda e, qfc=qfc, rb=rb: e.activation(
                    B_qT[:, qfc, :], bank(rb), AF.Copy, bias=0.0, scale=0.125),
                    wait=[qm_T])
            qT_A = WT("A")
            for qfc in range(8):
                rb = qfc % 2
                w = [qT_A] if qfc == 0 else []
                w.append(("V", max(0, sch.cnt["V"] - 2)))
                OP("T", lambda e, qfc=qfc, rb=rb: e.matmul(
                    bank(rb), prope_sb, B_qT[:, qfc, :], start=True, stop=True),
                   wait=w, inc="T")
                swq_T = WT("T")
                dve(lambda e, qfc=qfc, tl=tl: e.tensor_tensor(
                    B_xn3T[:, 0, :], B_qT[:, qfc, :],
                    B_cos[:, 0, tl * 512:(tl + 1) * 512], op=ALU.mult), wait=[swq_T])
                dve(lambda e, rb=rb, tl=tl: e.tensor_tensor(
                    B_xn3T[:, 1, :], bank(rb),
                    B_sin[:, 0, tl * 512:(tl + 1) * 512], op=ALU.mult))
                dve(lambda e, qfc=qfc: e.tensor_tensor(
                    B_qT[:, qfc, :], B_xn3T[:, 0, :], B_xn3T[:, 1, :], op=ALU.add))
            qrope_V = WT("V")

            for pg_l in range(2):
                pg = tl * 2 + pg_l
                kmax_c = 2 * (pg + 1)
                for h in range(NH_B):
                    mile_eb = {}

                    def emit_avb(kc, h=h, kmax_c=kmax_c):
                        w_av = [("A", mile_eb[kc])]
                        if kc == 0:
                            w_av.append(("V", sch.cnt["V"]))
                        OP("T", lambda e, h=h, kc=kc, st=(kc == 0),
                           sp=(kc == kmax_c - 1):
                           e.matmul(bank(4)[0:65, 0:256],
                                    B_v[:, kc, h, :], B_At[:, kc % 2, :],
                                    start=st, stop=sp), wait=w_av, inc="T")

                    for kc in range(kmax_c):
                        rS = 2 + kc % 2
                        w = []
                        if h == 0 and kc == 0:
                            w = [qrope_V, v_done_A]
                        w.append(("A", max(0, sch.cnt["A"] - 1)))
                        OP("T", lambda e, h=h, kc=kc, rS=rS, pg_l=pg_l: e.matmul(
                            bank(rS)[:, 0:256],
                            B_kr[64 * (h % 2):64 * (h % 2) + 64, h // 2,
                                 kc * 128:(kc + 1) * 128],
                            B_qT[64 * (h % 2):64 * (h % 2) + 64, h // 2,
                                 pg_l * 256:(pg_l + 1) * 256],
                            start=True, stop=True), wait=w, inc="T")
                        s_T = WT("T")
                        act(lambda e, kc=kc, rS=rS: e.activation(
                            B_At[:, kc % 2, :], bank(rS)[:, 0:256], AF.Exp,
                            bias=stat[:, 15:16], scale=1.0), wait=[s_T])
                        mile_eb[kc] = sch.cnt["A"]
                        if kc >= 1:
                            emit_avb(kc - 1)
                    emit_avb(kmax_c - 1)
                    av_T = WT("T")

                    def _recip_lp(e):
                        with nc.allow_low_precision(reason="bf16 softmax normaliser"):
                            return e.reciprocal(rrow_sb[0:1, 0:256],
                                                bank(4)[64:65, 0:256])
                    dve(_recip_lp, wait=[av_T])
                    rr_V = WT("V")
                    OP("T", lambda e: e.matmul(bank(6)[0:64, 0:256], ones_bf[0:1, 0:64],
                                               rrow_sb[0:1, 0:256], start=True, stop=True),
                       wait=[rr_V], inc="T")
                    rb_T = WT("T")
                    dve(lambda e: e.tensor_copy(B_rbx[0:64, 0:256],
                                                bank(6)[0:64, 0:256]),
                        wait=[rb_T])
                    dve(lambda e, h=h: e.tensor_tensor(
                        B_aoT[64 * (h % 2):64 * (h % 2) + 64, h // 2, :],
                        bank(4)[0:64, 0:256], B_rbx[0:64, 0:256], op=ALU.mult))
                ao_V = WT("V")
                for tc2 in range(2):
                    tc = pg * 2 + tc2
                    for hh in range(2):
                        w = [ao_V, b2w_G] if (tc2 == 0 and hh == 0) else []
                        if tc2 == 1 and hh == 0:
                            w.append(("V", sch.cnt["V"]))
                        mm_group(bank(hh),
                                 [(B_aoT[:, e1c, tc2 * 128:(tc2 + 1) * 128],
                                   B_wo[:, e1c, hh * 512:(hh + 1) * 512])
                                  for e1c in range(8)], wait=w)
                    wo_T = WT("T")
                    dve(lambda e, tc=tc: e.tensor_tensor(
                        x_res[:, tc, :], bank(0, 2), x_res[:, tc, :], op=ALU.add),
                        wait=[wo_T])
                pg_V = WT("V")
                OP("T", lambda e: None, wait=[pg_V])
        barrier()

    if n_sub >= 4:
        emit_ffn()
        barrier()

    final_w = sch.barrier_vals()
    for i in range(4):
        dma_s(yd_t[:, 4 * i:4 * i + 4, :], x_res[:, 4 * i:4 * i + 4, :],
              wait=(final_w if i == 0 else ()))
    if dbg:
        dmp_slab = nc.dram_tensor("dmp_slab", [128, SLAB_HALF], BF16,
                                  kind="ExternalOutput").ap()
        dmp_stat = nc.dram_tensor("dmp_stat", [128, 16 + NTC * 2], F32,
                                  kind="ExternalOutput").ap()
        for i in range(8):
            n8 = SLAB_HALF // 8
            dma_s(dmp_slab[:, i * n8:(i + 1) * n8], slab[:, i * n8:(i + 1) * n8],
                  wait=(final_w if i == 0 else ()))
        dma_s(dmp_stat[:, 0:16], stat)
        dma_s(dmp_stat[:, 16:16 + NTC * 2],
              lnst.rearrange("p a b -> p (a b)"))
    OP("S", lambda e: None, wait=[WT("S")])

    block = es.enter_context(nc.Block())

    def emit_engine(code, eng):
        last_wait = {}
        own = 0
        for engine, fn, waits, inc, inc_k in sch.ops:
            if engine != code:
                continue
            allw = list(waits)
            if code in ("V", "A") and own > 0:
                allw.append((code, own))
            for (sname, val) in allw:
                if val <= 0:
                    continue
                if last_wait.get(sname, -1) >= val:
                    continue
                eng.wait_ge(sems[sname], val)
                last_wait[sname] = val
            instr = fn(eng)
            if inc is not None:
                assert instr is not None
                instr.then_inc(sems[inc], inc_k)
                if code in ("V", "A"):
                    own += inc_k

    @block.sync
    def _(e):
        emit_engine("S", e)

    @block.gpsimd
    def _(e):
        emit_engine("G", e)

    @block.tensor
    def _(e):
        emit_engine("T", e)

    @block.scalar
    def _(e):
        emit_engine("A", e)

    @block.vector
    def _(e):
        emit_engine("V", e)

    es.close()
    return nc


def _consts():
    ident = np.eye(128, dtype=np.float32)
    onesm = np.ones((128, 128), dtype=np.float32)
    prope = np.zeros((128, 128), dtype=np.float32)
    for i in range(64):
        prope[2 * i + 1, 2 * i] = -1.0
        prope[2 * i, 2 * i + 1] = 1.0
    inv = 1.0 / (10000.0 ** (np.arange(0, 64, 2, dtype=np.float32) / 64.0))
    ang = np.outer(np.arange(T, dtype=np.float32), inv)
    cos = np.cos(ang).astype(np.float32)
    sin = np.sin(ang).astype(np.float32)
    fi = (np.arange(128) % 64) // 2
    rcos = np.ascontiguousarray(cos[:, fi].T)
    rsin = np.ascontiguousarray(sin[:, fi].T)
    triu = np.triu(np.ones((128, 128), dtype=np.float32), 1)
    iotag = np.zeros((128, 512), dtype=np.float32)
    iotag[:, 0:384] = np.arange(384, dtype=np.float32)[None, :]
    iotag[:, 384:512] = np.tile(np.arange(8, dtype=np.float32), 16)[None, :]
    jcol = (np.arange(24, dtype=np.float32)[None, :] * 128.0
            + np.arange(128, dtype=np.float32)[:, None])
    jcol = np.ascontiguousarray(jcol)
    return ident, onesm, prope, rcos, rsin, triu, iotag, jcol


_prog_cache = {}


def _get_prog(n_sub=4):
    if n_sub not in _prog_cache:
        _prog_cache[n_sub] = build_program(n_sub)
    return _prog_cache[n_sub]


def kernel(L0_attn_a_x, L0_attn_a_Mq, L0_attn_a_Mk, L0_attn_a_Mv,
           L0_attn_b_wq, L0_attn_b_w_down, L0_attn_b_wk_up, L0_attn_b_wv_up,
           L0_attn_b_wo, L0_ffn_w_gate, L0_ffn_b_gate, L0_ffn_W1, L0_ffn_b1,
           L0_ffn_W2, L0_ffn_b2, _n_sub=4):
    f = lambda a: np.ascontiguousarray(np.asarray(a, dtype=np.float32))
    x = f(L0_attn_a_x)
    ident, onesm, prope, rcos, rsin, triu, iotag, jcol = _consts()
    b1l = np.ascontiguousarray(
        f(L0_ffn_b1).reshape(NE, 32, 128).transpose(2, 0, 1).reshape(128, NE * 32))
    shared = dict(
        Mq=f(L0_attn_a_Mq), Mk=f(L0_attn_a_Mk), Mv=f(L0_attn_a_Mv),
        wq=f(L0_attn_b_wq), wdown=f(L0_attn_b_w_down), wkup=f(L0_attn_b_wk_up),
        wvup=f(L0_attn_b_wv_up), wo=f(L0_attn_b_wo),
        wgate=f(L0_ffn_w_gate),
        bgate=np.ascontiguousarray(np.tile(f(L0_ffn_b_gate).reshape(1, NE), (128, 1))),
        W1=f(L0_ffn_W1), b1l=b1l, W2=f(L0_ffn_W2), b2=f(L0_ffn_b2),
        ident=ident, onesm=onesm, prope=prope, rcos=rcos, rsin=rsin,
        triu=triu, iotag=iotag, jcol=jcol,
    )
    nc = _get_prog(_n_sub)
    in_maps = [dict(shared, x=x[c]) for c in range(8)]
    res = run_bass_kernel_spmd(nc, in_maps, core_ids=list(range(8)))
    return np.stack([res.results[c]["y"] for c in range(8)], axis=0)
